# version 01
# baseline (speedup 1.0000x reference)
import os
import numpy as np
import ml_dtypes
import concourse.bass as bass
import concourse.mybir as mybir
from concourse.bass_utils import run_bass_kernel_spmd

F32 = mybir.dt.float32
BF16 = mybir.dt.bfloat16
I32 = mybir.dt.int32
OP = mybir.AluOpType
AF = mybir.ActivationFunctionType
AX = mybir.AxisListType

NCORES = 8
T = 2048
D = 2048
NMLP = 16384
NPAN = 128
NPAN_LOC = 16
CHUNK = 64
NCH = 32
NH, NKV, DH = 16, 4, 128
GELU_C = 0.044715
GELU_S = 2.0 * 0.7978845608028654
EPS = 1e-5

_S = os.environ.get("KSTAGE", "4")
DO_DEC = _S in ("1", "gu", "2", "3", "4")
DO_LN1 = _S in ("ln", "gu", "2", "3", "4")
DO_GU1 = _S in ("gu", "2", "3", "4")
DO_DN1 = _S in ("2", "3", "4")
STAGE = {"1": 1, "ln": 1, "gu": 1, "2": 2, "3": 3, "4": 4}[_S]
DEBUG = os.environ.get("KDEBUG", "") == "1"

ENGS = ("sync", "vector", "scalar", "tensor", "gpsimd")


class Sem:
    __slots__ = ("h", "v", "name")

    def __init__(self, es, nc, name):
        self.h = es.enter_context(nc.semaphore(name))
        self.v = 0
        self.name = name

    def inc(self, n=1):
        self.v += n
        return self.v


class Prog:
    def __init__(self):
        self.q = {k: [] for k in ENGS}

    def em(self, eng, fn):
        self.q[eng].append(fn)

    def wait(self, eng, sem, val=None):
        v = sem.v if val is None else val
        if v <= 0:
            return
        self.q[eng].append(lambda e, s=sem.h, t=v: e.wait_ge(s, t))

    def barrier(self, sems):
        for eng in ENGS:
            s = sems[eng]
            s.inc(1)
            if eng == "sync":
                self.q[eng].append(lambda e, sh=s.h: e.nop().then_inc(sh, 1))
            else:
                self.q[eng].append(lambda e, sh=s.h: e.drain().then_inc(sh, 1))
        for eng in ENGS:
            for o in ENGS:
                if o != eng:
                    self.wait(eng, sems[o])


def build_nc():
    nc = bass.Bass("TRN2", target_bir_lowering=False)
    from contextlib import ExitStack
    es = ExitStack()
    p = Prog()

    def din(name, shape, dt):
        return nc.dram_tensor(name, shape, dt, kind="ExternalInput").ap()

    x_in = din("x_in", [T, D], F32)
    wqT = din("wqT", [D, NH * DH], F32)
    wkT = din("wkT", [D, NKV * DH], F32)
    wvT = din("wvT", [D, NKV * DH], F32)
    woT = din("woT", [NH * DH, D], F32)
    pk_g = din("pk_g", [T, 1024], I32)
    pk_u = din("pk_u", [T, 1024], I32)
    pk_d = din("pk_d", [D, 1024], I32)
    sc_g = din("sc_g", [T, 64], F32)
    sc_u = din("sc_u", [T, 64], F32)
    sc_d = din("sc_d", [D, 64], F32)
    ident = din("ident", [128, 128], BF16)
    cos_in = din("cos_t", [DH, T], F32)
    sin_in = din("sin_t", [DH, T], F32)
    mask_in = din("maskT", [CHUNK, NH * CHUNK], BF16)
    qdec_in = din("qdec_t", [128, NH * CHUNK], BF16)
    kdec_in = din("kdec_t", [128, NH], F32)
    cw_in = din("cw_t", [128, NCORES * NH], F32)

    out = nc.dram_tensor("out", [T, D], F32, kind="ExternalOutput").ap()

    dbg = {}
    if DEBUG:
        if DO_LN1:
            dbg["y"] = nc.dram_tensor("dbg_y", [16 * 128, D], BF16, kind="ExternalOutput").ap()
        if DO_GU1:
            dbg["inter"] = nc.dram_tensor("dbg_inter", [2048, T], BF16, kind="ExternalOutput").ap()
        if DO_DN1:
            dbg["x1"] = nc.dram_tensor("dbg_x1", [T, D], F32, kind="ExternalOutput").ap()
        if STAGE >= 3:
            dbg["x2"] = nc.dram_tensor("dbg_x2", [T, D], F32, kind="ExternalOutput").ap()

    def dint(name, shape, dt):
        return nc.dram_tensor(name, shape, dt, kind="Internal").ap()

    cl_g = dint("cl_g", [NPAN_LOC * 128, D], BF16)
    cl_u = dint("cl_u", [NPAN_LOC * 128, D], BF16)
    cl_d = dint("cl_d", [NPAN_LOC * 128, D], BF16)
    cf_g = dint("cf_g", [NPAN * 128, D], BF16)
    cf_u = dint("cf_u", [NPAN * 128, D], BF16)
    cf_d = dint("cf_d", [NPAN * 128, D], BF16)
    inter_d = dint("inter_d", [NMLP, T], BF16)
    x1_d = dint("x1_d", [T, D], F32)
    x2_d = dint("x2_d", [T, D], F32)
    y2_d = dint("y2_d", [16 * 128, T], BF16)
    wq_bf = dint("wq_bf", [D, NH * DH], BF16)
    wo_bf = dint("wo_bf", [NH * DH, D], BF16)
    kvloc_d = dint("kvloc_d", [NH * DH, DH], F32)
    kvall_d = dint("kvall_d", [NCORES * NH * DH, DH], F32)

    sems = {}

    def S(name):
        if name in sems:
            return sems[name]
        s = Sem(es, nc, name)
        sems[name] = s
        return s

    bar = {eng: S("bar_" + eng) for eng in ENGS}

    def sb_scope(specs):
        guards = {}
        ts = {}
        for name, shape, dt in specs:
            g = nc.sbuf_tensor(name, shape, dt)
            ts[name] = g.__enter__()
            guards[name] = g
        def rel():
            for g in reversed(list(guards.values())):
                g.__exit__(None, None, None)
        return ts, rel

    def ps_scope(specs):
        guards = {}
        ts = {}
        for name, shape, dt in specs:
            g = nc.psum_tensor(name, shape, dt)
            ts[name] = g.__enter__()
            guards[name] = g
        def rel():
            for g in reversed(list(guards.values())):
                g.__exit__(None, None, None)
        return ts, rel

    def dma(eng, out_ap, in_ap, sem):
        p.em(eng, lambda e, o=out_ap, i=in_ap, s=sem.h: e.dma_start(out=o, in_=i).then_inc(s, 16))
        return sem.inc(16)

    big_buf = es.enter_context(nc.sbuf_tensor("big_buf", [128, 16 * 2048], BF16))
    id_sb = es.enter_context(nc.sbuf_tensor("id_sb", [128, 128], BF16))

    s_const = S("s_const")
    dma("sync", id_sb[:, :], ident, s_const)

    dec_specs = [("pk_sb0", [128, 1024], I32), ("pk_sb1", [128, 1024], I32),
                 ("sc_sb0", [128, 64], F32), ("sc_sb1", [128, 64], F32),
                 ("dec_out0", [128, 2048], BF16), ("dec_out1", [128, 2048], BF16),
                 ("pan_buf0", [128, 2048], BF16), ("pan_buf1", [128, 2048], BF16)]
    dec_tmp_names = ["lo_i", "wf", "xv", "bv", "mv", "e2", "e4", "e6", "zv", "uv", "tv", "fac", "pw", "val", "sgn"]
    for k in dec_tmp_names:
        dec_specs.append(("dx_" + k, [128, 1024], I32 if k == "lo_i" else F32))
    dsb, dec_rel = sb_scope(dec_specs)
    dps, dec_ps_rel = ps_scope([(f"ps_tp{i}", [128, 128], BF16) for i in range(4)])
    ps_tp = [dps[f"ps_tp{i}"] for i in range(4)]
    pk_sb = [dsb["pk_sb0"], dsb["pk_sb1"]]
    sc_sb = [dsb["sc_sb0"], dsb["sc_sb1"]]
    dec_out = [dsb["dec_out0"], dsb["dec_out1"]]
    pan_buf = [dsb["pan_buf0"], dsb["pan_buf1"]]

    s_pk = [S("s_pk0"), S("s_pk1")]
    s_dec = S("s_dec"); s_tp = S("s_tp"); s_tpe = S("s_tpe")
    s_cw = [S("s_cw0"), S("s_cw1")]
    s_gat = S("s_gat")

    def emit_decode_tile(b, fold_half):
        def f(v):
            pi = pk_sb[b]
            lo_i = dsb["dx_lo_i"]; wf = dsb["dx_wf"]; x = dsb["dx_xv"]
            bv = dsb["dx_bv"]; m = dsb["dx_mv"]
            e2 = dsb["dx_e2"]; e4 = dsb["dx_e4"]; e6 = dsb["dx_e6"]; z = dsb["dx_zv"]
            u = dsb["dx_uv"]; t = dsb["dx_tv"]; fac = dsb["dx_fac"]
            pw = dsb["dx_pw"]; val = dsb["dx_val"]; sgn = dsb["dx_sgn"]
            ot = dec_out[b]; st = sc_sb[b]
            v.tensor_scalar(out=lo_i[:], in0=pi[:], scalar1=15, scalar2=None, op0=OP.bitwise_and)
            v.tensor_copy(out=wf[:], in_=pi[:])
            v.drain()
            ins = None
            for which, off in ((0, 0), (1, 1)):
                if which == 0:
                    v.tensor_copy(out=x[:], in_=lo_i[:])
                    s = 1.0
                else:
                    v.drain()
                    v.scalar_tensor_tensor(out=x[:], in0=x[:], scalar=-1.0, in1=wf[:], op0=OP.mult, op1=OP.add)
                    s = 16.0
                v.drain()
                v.tensor_scalar(out=bv[:], in0=x[:], scalar1=7.5 * s, scalar2=None, op0=OP.is_gt)
                v.drain()
                v.scalar_tensor_tensor(out=m[:], in0=bv[:], scalar=-8.0 * s, in1=x[:], op0=OP.mult, op1=OP.add)
                v.drain()
                v.tensor_scalar(out=e2[:], in0=m[:], scalar1=1.5 * s, scalar2=None, op0=OP.is_gt)
                v.tensor_scalar(out=e4[:], in0=m[:], scalar1=3.5 * s, scalar2=None, op0=OP.is_gt)
                v.tensor_scalar(out=e6[:], in0=m[:], scalar1=5.5 * s, scalar2=None, op0=OP.is_gt)
                v.tensor_scalar(out=z[:], in0=m[:], scalar1=0.5 * s, scalar2=None, op0=OP.is_gt)
                v.tensor_scalar(out=t[:], in0=m[:], scalar1=0.5 / s, scalar2=1.0, op0=OP.mult, op1=OP.add)
                v.drain()
                v.tensor_tensor(out=u[:], in0=e2[:], in1=e4[:], op=OP.add)
                v.drain()
                v.tensor_tensor(out=u[:], in0=u[:], in1=e6[:], op=OP.add)
                v.drain()
                v.tensor_tensor(out=fac[:], in0=t[:], in1=u[:], op=OP.subtract)
                v.drain()
                v.scalar_tensor_tensor(out=fac[:], in0=z[:], scalar=-0.5, in1=fac[:], op0=OP.mult, op1=OP.add)
                v.drain()
                v.scalar_tensor_tensor(out=fac[:], in0=e2[:], scalar=0.5, in1=fac[:], op0=OP.mult, op1=OP.add)
                v.scalar_tensor_tensor(out=pw[:], in0=e2[:], scalar=0.5, in1=e4[:], op0=OP.mult, op1=OP.add)
                v.drain()
                v.scalar_tensor_tensor(out=pw[:], in0=e6[:], scalar=2.0, in1=pw[:], op0=OP.mult, op1=OP.add)
                v.drain()
                v.tensor_scalar(out=pw[:], in0=pw[:], scalar1=0.5, scalar2=None, op0=OP.add)
                v.tensor_scalar(out=sgn[:], in0=bv[:], scalar1=-2.0, scalar2=1.0, op0=OP.mult, op1=OP.add)
                v.drain()
                v.tensor_tensor(out=val[:], in0=pw[:], in1=fac[:], op=OP.mult)
                v.drain()
                v.tensor_tensor(out=val[:], in0=val[:], in1=z[:], op=OP.mult)
                v.drain()
                v.tensor_tensor(out=val[:], in0=val[:], in1=sgn[:], op=OP.mult)
                v.drain()
                st_rep = st[:, :].unsqueeze(2).broadcast_to([128, 64, 16])
                val3 = val[:, :].rearrange("p (a c) -> p a c", a=64)
                ot_v = ot[:, off::2].rearrange("p (a c) -> p a c", a=64)
                if fold_half:
                    ins = v.scalar_tensor_tensor(out=ot_v, in0=val3, scalar=0.5, in1=st_rep, op0=OP.mult, op1=OP.mult)
                else:
                    ins = v.tensor_tensor(out=ot_v, in0=val3, in1=st_rep, op=OP.mult)
            return ins
        return f

    def emit_decode_phase():
        jobs = [(pk_g, sc_g, cl_g, cf_g, False), (pk_u, sc_u, cl_u, cf_u, False),
                (pk_d, sc_d, cl_d, cf_d, True)]
        p.wait("tensor", s_const, 16)
        for wi, (pk, sc, cl, cf, fold) in enumerate(jobs):
            is_wd = (wi == 2)
            for rt in range(NPAN_LOC):
                j = wi * NPAN_LOC + rt
                b = j % 2
                p.wait("sync", s_dec, j - 1)
                dma("sync", pk_sb[b][:, :], pk[rt * 128:(rt + 1) * 128, :], s_pk[b])
                t_sc = dma("sync", sc_sb[b][:, :], sc[rt * 128:(rt + 1) * 128, :], s_pk[b])
                p.wait("vector", s_pk[b], t_sc)
                p.wait("vector", s_tp, (j - 1) * 16)
                fdec = emit_decode_tile(b, fold)
                p.em("vector", lambda v, f=fdec, s=s_dec.h: f(v).then_inc(s, 1))
                t_dec = s_dec.inc(1)
                p.wait("tensor", s_dec, t_dec)
                for dc in range(16):
                    ktp = s_tp.v
                    pb = ktp % 4
                    p.wait("tensor", s_tpe, ktp - 3)
                    p.em("tensor", lambda e, i=dec_out[b][:, dc * 128:(dc + 1) * 128], o=ps_tp[pb],
                         s=s_tp.h: e.transpose(o[:, :], i, id_sb[:, :]).then_inc(s, 1))
                    t_tp = s_tp.inc(1)
                    if is_wd:
                        dst = big_buf[:, dc * 2048 + rt * 128: dc * 2048 + rt * 128 + 128]
                    else:
                        dst = pan_buf[b][:, dc * 128:(dc + 1) * 128]
                        if dc == 0:
                            p.wait("scalar", s_cw[b], s_cw[b].v)
                    p.wait("scalar", s_tp, t_tp)
                    p.em("scalar", lambda e, i=ps_tp[pb], o=dst,
                         s=s_tpe.h: e.copy(out=o, in_=i[:, :]).then_inc(s, 1))
                    s_tpe.inc(1)
                if not is_wd:
                    p.wait("gpsimd", s_tpe, s_tpe.v)
                    dma("gpsimd", cl[rt * 128:(rt + 1) * 128, :], pan_buf[b][:, :], s_cw[b])
            if is_wd:
                p.wait("gpsimd", s_tpe, s_tpe.v)
                for ns in range(NPAN_LOC):
                    dma("gpsimd", cl[ns * 128:(ns + 1) * 128, :],
                        big_buf[:, ns * 2048:(ns + 1) * 2048], s_cw[ns % 2])
            p.wait("gpsimd", s_cw[0], s_cw[0].v)
            p.wait("gpsimd", s_cw[1], s_cw[1].v)
            p.em("gpsimd", lambda e, i=cl, o=cf, s=s_gat.h:
                 e.collective_compute("AllGather", OP.bypass,
                                      replica_groups=[list(range(NCORES))],
                                      ins=[i], outs=[o]).then_inc(s, 1))
            s_gat.inc(1)

    if DO_DEC:
        emit_decode_phase()
    p.barrier(bar)
    dec_ps_rel()
    dec_rel()

    if STAGE >= 3:
        wsb, wrel = sb_scope([("wc_f0", [128, 2048], F32), ("wc_f1", [128, 2048], F32),
                              ("wc_b0", [128, 2048], BF16), ("wc_b1", [128, 2048], BF16)])
        s_wcl = [S("s_wcl0"), S("s_wcl1")]
        s_wcc = S("s_wcc"); s_wcs = [S("s_wcs0"), S("s_wcs1")]
        for wi, (src, dst) in enumerate(((wqT, wq_bf), (woT, wo_bf))):
            for rt in range(16):
                j = wi * 16 + rt
                b = j % 2
                p.wait("sync", s_wcc, j - 1)
                t_l = dma("sync", wsb[f"wc_f{b}"][:, :], src[rt * 128:(rt + 1) * 128, :], s_wcl[b])
                p.wait("vector", s_wcl[b], t_l)
                p.wait("vector", s_wcs[b], s_wcs[b].v)
                p.em("vector", lambda v, i=wsb[f"wc_f{b}"], o=wsb[f"wc_b{b}"], s=s_wcc.h:
                     v.tensor_copy(out=o[:, :], in_=i[:, :]).then_inc(s, 1))
                s_wcc.inc(1)
                p.wait("gpsimd", s_wcc, s_wcc.v)
                dma("gpsimd", dst[rt * 128:(rt + 1) * 128, :], wsb[f"wc_b{b}"][:, :], s_wcs[b])
        p.wait("gpsimd", s_wcs[0], s_wcs[0].v)
        p.wait("gpsimd", s_wcs[1], s_wcs[1].v)
        p.barrier(bar)
        wrel()

    def emit_ln(tag, src_dram, mirror):
        lsb, lrel = sb_scope([
            (f"{tag}_xs0", [128, 2048], F32), (f"{tag}_xs1", [128, 2048], F32),
            (f"{tag}_xc", [128, 2048], F32), (f"{tag}_sq", [128, 2048], F32),
            (f"{tag}_st", [128, 8], F32),
            (f"{tag}_yt0", [128, 2048], BF16), (f"{tag}_yt1", [128, 2048], BF16),
        ])
        lps, lpsrel = ps_scope([(f"{tag}_pst{i}", [128, 128], BF16) for i in range(4)])
        pst = [lps[f"{tag}_pst{i}"] for i in range(4)]
        s_lx = [S("ln_lx0"), S("ln_lx1")]
        s_ly = S("ln_ly"); s_lsq = S("ln_lsq")
        s_lrs = S("ln_lrs"); s_ltp = S("ln_ltp"); s_lte = S("ln_lte")
        s_lm = S("ln_lm")
        b_ly = s_ly.v; b_ltp = s_ltp.v
        xc, sq, st = lsb[f"{tag}_xc"], lsb[f"{tag}_sq"], lsb[f"{tag}_st"]
        for ts in range(16):
            b = ts % 2
            xs = lsb[f"{tag}_xs{b}"]
            yt = lsb[f"{tag}_yt{b}"]
            p.wait("sync", s_ly, b_ly + ts - 1)
            t_x = dma("sync", xs[:, :], src_dram[ts * 128:(ts + 1) * 128, :], s_lx[b])
            p.wait("vector", s_lx[b], t_x)

            def ln_chain(v, xs=xs, yt=yt, xc=xc, sq=sq, st=st):
                v.tensor_reduce(out=st[:, 0:1], in_=xs[:, :], axis=AX.X, op=OP.add)
                v.drain()
                v.tensor_scalar(out=st[:, 1:2], in0=st[:, 0:1], scalar1=1.0 / D, scalar2=None, op0=OP.mult)
                v.drain()
                return v.tensor_scalar(out=xc[:, :], in0=xs[:, :], scalar1=st[:, 1:2], scalar2=None, op0=OP.subtract)
            p.em("vector", lambda v, f=ln_chain, s=s_lsq.h: f(v).then_inc(s, 1))
            t_sq = s_lsq.inc(1)
            p.wait("scalar", s_lsq, t_sq)
            p.em("scalar", lambda e, s=s_lsq.h: e.activation(out=sq[:, :], in_=xc[:, :], func=AF.Square).then_inc(s, 1))
            t_sq2 = s_lsq.inc(1)
            p.wait("vector", s_lsq, t_sq2)

            def var_chain(v, sq=sq, st=st):
                v.tensor_reduce(out=st[:, 2:3], in_=sq[:, :], axis=AX.X, op=OP.add)
                v.drain()
                return v.tensor_scalar(out=st[:, 3:4], in0=st[:, 2:3], scalar1=1.0 / D, scalar2=EPS, op0=OP.mult, op1=OP.add)
            p.em("vector", lambda v, f=var_chain, s=s_lrs.h: f(v).then_inc(s, 1))
            t_v = s_lrs.inc(1)
            p.wait("scalar", s_lrs, t_v)
            p.em("scalar", lambda e, s=s_lrs.h: e.activation(out=st[:, 4:5], in_=st[:, 3:4], func=AF.Sqrt).then_inc(s, 1))
            t_s = s_lrs.inc(1)
            p.wait("vector", s_lrs, t_s)

            p.wait("vector", s_ltp, b_ltp + 16 * (ts - 1))

            def y_chain(v, yt=yt, xc=xc, st=st):
                v.reciprocal(out=st[:, 5:6], in_=st[:, 4:5])
                v.drain()
                return v.tensor_scalar(out=yt[:, :], in0=xc[:, :], scalar1=st[:, 5:6], scalar2=None, op0=OP.mult)
            p.em("vector", lambda v, f=y_chain, s=s_ly.h: f(v).then_inc(s, 1))
            t_y = s_ly.inc(1)
            p.wait("tensor", s_ly, t_y)
            for dc in range(16):
                ktp = s_ltp.v
                pb = ktp % 4
                p.wait("tensor", s_lte, ktp - 3)
                p.em("tensor", lambda e, i=yt[:, dc * 128:(dc + 1) * 128], o=pst[pb],
                     s=s_ltp.h: e.transpose(o[:, :], i, id_sb[:, :]).then_inc(s, 1))
                t_tp = s_ltp.inc(1)
                p.wait("scalar", s_ltp, t_tp)
                p.em("scalar", lambda e, i=pst[pb],
                     o=big_buf[:, dc * 2048 + ts * 128: dc * 2048 + ts * 128 + 128],
                     s=s_lte.h: e.copy(out=o, in_=i[:, :]).then_inc(s, 1))
                s_lte.inc(1)
        if mirror:
            p.wait("gpsimd", s_lte, s_lte.v)
            for dc in range(16):
                dma("gpsimd", y2_d[dc * 128:(dc + 1) * 128, :], big_buf[:, dc * 2048:(dc + 1) * 2048], s_lm)
            p.wait("gpsimd", s_lm, s_lm.v)
        p.barrier(bar)
        lpsrel()
        lrel()

    def emit_gateup(tag):
        gsb, grel = sb_scope([
            (f"{tag}_wg0", [128, 2048], BF16), (f"{tag}_wg1", [128, 2048], BF16),
            (f"{tag}_wu0", [128, 2048], BF16), (f"{tag}_wu1", [128, 2048], BF16),
            (f"{tag}_sq0", [128, 512], F32), (f"{tag}_sq1", [128, 512], F32),
            (f"{tag}_g30", [128, 512], F32), (f"{tag}_g31", [128, 512], F32),
            (f"{tag}_z00", [128, 512], F32), (f"{tag}_z01", [128, 512], F32),
            (f"{tag}_t10", [128, 512], F32), (f"{tag}_t11", [128, 512], F32),
            (f"{tag}_w10", [128, 512], F32), (f"{tag}_w11", [128, 512], F32),
            (f"{tag}_it0", [128, 512], BF16), (f"{tag}_it1", [128, 512], BF16),
        ])
        gps, gpsrel = ps_scope([(f"{tag}_pg0", [128, 512], F32), (f"{tag}_pg1", [128, 512], F32),
                                (f"{tag}_pu0", [128, 512], F32), (f"{tag}_pu1", [128, 512], F32)])
        pg = [gps[f"{tag}_pg0"], gps[f"{tag}_pg1"]]
        pu = [gps[f"{tag}_pu0"], gps[f"{tag}_pu1"]]
        s_wl = [S("gu_wl0"), S("gu_wl1")]
        s_mm = S("gu_mm"); s_sq = S("gu_sq")
        s_z0 = S("gu_z0"); s_t1 = S("gu_t1"); s_ge = S("gu_ge")
        s_iw = [S("gu_iw0"), S("gu_iw1")]
        b_mm = s_mm.v; b_ge = s_ge.v; b_z0 = s_z0.v; b_t1 = s_t1.v
        p.wait("sync", s_gat, 2)
        for pan in range(NPAN):
            b = pan % 2
            wg = gsb[f"{tag}_wg{b}"]; wu = gsb[f"{tag}_wu{b}"]
            p.wait("sync", s_mm, b_mm + (pan - 1) * 8)
            dma("sync", wg[:, :], cf_g[pan * 128:(pan + 1) * 128, :], s_wl[b])
            t_w = dma("sync", wu[:, :], cf_u[pan * 128:(pan + 1) * 128, :], s_wl[b])
            for tt in range(4):
                k = pan * 4 + tt
                bb = k % 2
                psg = pg[bb]; psu = pu[bb]
                if tt == 0:
                    p.wait("tensor", s_wl[b], t_w)
                p.wait("tensor", s_ge, b_ge + k - 1)
                for dc in range(16):
                    def mmg(e, o=psg, l=wg[:, dc * 128:(dc + 1) * 128],
                            r=big_buf[:, dc * 2048 + tt * 512: dc * 2048 + (tt + 1) * 512],
                            st=(dc == 0), sp=(dc == 15), s=s_mm.h):
                        ins = e.matmul(o[:, :], l, r, start=st, stop=sp)
                        if sp:
                            ins.then_inc(s, 1)
                        return ins
                    p.em("tensor", mmg)
                s_mm.inc(1)
                for dc in range(16):
                    def mmu(e, o=psu, l=wu[:, dc * 128:(dc + 1) * 128],
                            r=big_buf[:, dc * 2048 + tt * 512: dc * 2048 + (tt + 1) * 512],
                            st=(dc == 0), sp=(dc == 15), s=s_mm.h):
                        ins = e.matmul(o[:, :], l, r, start=st, stop=sp)
                        if sp:
                            ins.then_inc(s, 1)
                        return ins
                    p.em("tensor", mmu)
                t_mmu = s_mm.inc(1)
                sqb = gsb[f"{tag}_sq{bb}"]; g3 = gsb[f"{tag}_g3{bb}"]
                z0 = gsb[f"{tag}_z0{bb}"]; t1 = gsb[f"{tag}_t1{bb}"]
                w1 = gsb[f"{tag}_w1{bb}"]; itb = gsb[f"{tag}_it{bb}"]
                p.wait("scalar", s_mm, t_mmu - 1)
                p.wait("scalar", s_z0, b_z0 + k - 1)
                p.em("scalar", lambda e, o=sqb, i=psg, s=s_sq.h:
                     e.activation(out=o[:, :], in_=i[:, :], func=AF.Square).then_inc(s, 1))
                t_sq = s_sq.inc(1)
                p.wait("vector", s_sq, t_sq)
                p.wait("vector", s_t1, b_t1 + k - 1)

                def zf(v, sqb=sqb, g3=g3, z0=z0, psg=psg):
                    v.tensor_tensor(out=g3[:, :], in0=sqb[:, :], in1=psg[:, :], op=OP.mult)
                    v.drain()
                    return v.scalar_tensor_tensor(out=z0[:, :], in0=g3[:, :], scalar=GELU_C, in1=psg[:, :], op0=OP.mult, op1=OP.add)
                p.em("vector", lambda v, f=zf, s=s_z0.h: f(v).then_inc(s, 1))
                t_z0 = s_z0.inc(1)
                p.wait("scalar", s_z0, t_z0)
                p.em("scalar", lambda e, o=t1, i=z0, s=s_t1.h:
                     e.activation(out=o[:, :], in_=i[:, :], func=AF.Sigmoid, scale=GELU_S).then_inc(s, 1))
                t_t1 = s_t1.inc(1)
                p.wait("vector", s_t1, t_t1)
                p.wait("vector", s_mm, t_mmu)
                p.wait("vector", s_iw[bb], s_iw[bb].v)

                def gf(v, t1=t1, w1=w1, itb=itb, psg=psg, psu=psu):
                    v.tensor_tensor(out=w1[:, :], in0=t1[:, :], in1=psg[:, :], op=OP.mult)
                    v.drain()
                    return v.tensor_tensor(out=itb[:, :], in0=w1[:, :], in1=psu[:, :], op=OP.mult)
                p.em("vector", lambda v, f=gf, s=s_ge.h: f(v).then_inc(s, 1))
                s_ge.inc(1)
                p.wait("gpsimd", s_ge, s_ge.v)
                dma("gpsimd", inter_d[pan * 128:(pan + 1) * 128, tt * 512:(tt + 1) * 512], itb[:, :], s_iw[bb])
        p.wait("gpsimd", s_iw[0], s_iw[0].v)
        p.wait("gpsimd", s_iw[1], s_iw[1].v)
        p.barrier(bar)
        gpsrel()
        grel()

    def emit_down(tag, src_res, dst):
        dnsb, dnrel = sb_scope([
            (f"{tag}_acc", [128, 8 * 2048], F32),
            (f"{tag}_it", [128, 16 * 1024], BF16),
        ])
        dn_ps, dn_psrel = ps_scope([(f"{tag}_pa", [128, 2048], F32), (f"{tag}_pb", [128, 2048], F32)])
        psd = [dn_ps[f"{tag}_pa"], dn_ps[f"{tag}_pb"]]
        acc = dnsb[f"{tag}_acc"]; itb = dnsb[f"{tag}_it"]
        s_ai = S("dn_ai")
        s_ld = [S("dn_ld0"), S("dn_ld1")]
        s_fl = S("dn_fl"); s_st = S("dn_st"); s_mmdn = S("dn_mmdn")
        b_st = s_st.v; b_mmdn = s_mmdn.v; b_fl = s_fl.v
        p.wait("sync", s_gat, 3)
        for th in range(2):
            p.wait("sync", s_st, b_st + 16 * 8 * th)
            for ts in range(8):
                dma("sync", acc[:, ts * 2048:(ts + 1) * 2048],
                    src_res[(th * 8 + ts) * 128:(th * 8 + ts + 1) * 128, :], s_ai)
            t_ai = s_ai.v
            for sub in range(16):
                gsub = th * 16 + sub
                sl0 = (sub % 2) * 8
                p.wait("sync", s_mmdn, b_mmdn + 8 * (gsub - 1))
                for pp in range(8):
                    pan = sub * 8 + pp
                    dma("sync", big_buf[:, (sl0 + pp) * 2048:(sl0 + pp + 1) * 2048],
                        cf_d[pan * 128:(pan + 1) * 128, :], s_ld[sub % 2])
                    t_ld = dma("sync", itb[:, (sl0 + pp) * 1024:(sl0 + pp + 1) * 1024],
                               inter_d[pan * 128:(pan + 1) * 128, th * 1024:(th + 1) * 1024], s_ld[sub % 2])
                p.wait("tensor", s_ld[sub % 2], t_ld)
                for tb in range(8):
                    kt = gsub * 8 + tb
                    bb = tb % 2
                    ps = psd[bb]
                    p.wait("tensor", s_fl, b_fl + kt - 1)
                    for pp in range(8):
                        for hh in range(4):
                            def mmd(e, o=ps[:, hh * 512:(hh + 1) * 512],
                                    l=itb[:, (sl0 + pp) * 1024 + tb * 128: (sl0 + pp) * 1024 + (tb + 1) * 128],
                                    r=big_buf[:, (sl0 + pp) * 2048 + hh * 512: (sl0 + pp) * 2048 + (hh + 1) * 512],
                                    st=(pp == 0), sp=(pp == 7), fin=(pp == 7 and hh == 3), s=s_mmdn.h):
                                ins = e.matmul(o, l, r, start=st, stop=sp)
                                if fin:
                                    ins.then_inc(s, 1)
                                return ins
                            p.em("tensor", mmd)
                    t_mm = s_mmdn.inc(1)
                    p.wait("vector", s_mmdn, t_mm)
                    if sub == 0 and tb == 0:
                        p.wait("vector", s_ai, t_ai)

                    def ff(v, ps=ps, a=acc[:, tb * 2048:(tb + 1) * 2048]):
                        return v.scalar_tensor_tensor(out=a, in0=ps[:, :], scalar=0.0, in1=a, op0=OP.bypass, op1=OP.add)
                    p.em("vector", lambda v, f=ff, s=s_fl.h: f(v).then_inc(s, 1))
                    s_fl.inc(1)
            p.wait("gpsimd", s_fl, s_fl.v)
            for ts in range(8):
                dma("gpsimd", dst[(th * 8 + ts) * 128:(th * 8 + ts + 1) * 128, :],
                    acc[:, ts * 2048:(ts + 1) * 2048], s_st)
        p.wait("gpsimd", s_st, s_st.v)
        p.barrier(bar)
        dn_psrel()
        dnrel()
        return s_st

    def emit_attention():
        asb, arel = sb_scope([
            ("at_wk", [128, 16 * 512], BF16), ("at_wv", [128, 16 * 512], BF16),
            ("at_kfm", [128, 4 * 2048], BF16), ("at_ktm", [128, 4 * 2048], BF16),
            ("at_vtm", [128, 16 * 512], BF16),
            ("at_kraw", [128, 512], F32), ("at_krot", [128, 512], F32),
            ("at_cos", [128, 512], F32), ("at_sin", [128, 512], F32),
            ("at_kv", [128, 16 * 128], F32), ("at_kvb", [128, 16 * 128], BF16),
            ("at_qb", [128, 16 * 512], BF16),
            ("at_ktd", [128, 4 * 128], BF16),
            ("at_sc", [128, 4 * 64], BF16), ("at_qd", [128, 4 * 64], BF16),
            ("at_x1", [128, 2048], F32), ("at_x2", [128, 2048], F32),
            ("at_msk", [64, NH * 64], BF16), ("at_qdc", [128, NH * 64], BF16),
            ("at_kdc", [128, NH], F32), ("at_cw", [128, NCORES * NH], F32),
        ])
        aps1, aps1rel = ps_scope([
            ("ap_q0", [128, 512], F32), ("ap_q1", [128, 512], F32),
            ("ap_kv0", [128, 128], F32), ("ap_kv1", [128, 128], F32),
        ])
        aps_tp, apstprel = ps_scope([("ap_tp0", [128, 128], BF16), ("ap_tp1", [128, 128], BF16)])
        wk_sb, wv_sb = asb["at_wk"], asb["at_wv"]
        kfm, ktm, vtm = asb["at_kfm"], asb["at_ktm"], asb["at_vtm"]
        kraw, krot = asb["at_kraw"], asb["at_krot"]
        cos_sb, sin_sb = asb["at_cos"], asb["at_sin"]
        kv_sb, kv_bf = asb["at_kv"], asb["at_kvb"]
        q_blk = asb["at_qb"]
        y2b = asb["at_wk"]
        o_blk = asb["at_wv"]
        ktd = asb["at_ktd"]; scb = asb["at_sc"]; qdb = asb["at_qd"]
        x1sb, x2sb = asb["at_x1"], asb["at_x2"]
        kvr = x1sb
        msk, qdc, kdc, cw_sb = asb["at_msk"], asb["at_qdc"], asb["at_kdc"], asb["at_cw"]
        psq = [aps1["ap_q0"], aps1["ap_q1"]]
        pskv = [aps1["ap_kv0"], aps1["ap_kv1"]]
        pstp = [aps_tp["ap_tp0"], aps_tp["ap_tp1"]]

        s_cl = S("a_cl")
        dma("sync", msk[:, :], mask_in, s_cl)
        dma("sync", qdc[:, :], qdec_in, s_cl)
        dma("sync", kdc[:, :], kdec_in, s_cl)
        t_cl = dma("sync", cw_sb[:, :], cw_in, s_cl)

        s_wkl = [S("a_wkl0"), S("a_wkl1")]
        s_wkc = S("a_wkc")
        for wi, (src_w, dstb) in enumerate(((wkT, wk_sb), (wvT, wv_sb))):
            for dc in range(16):
                j = wi * 16 + dc
                b = j % 2
                stage = kraw if b == 0 else krot
                p.wait("sync", s_wkc, j - 1)
                t_l = dma("sync", stage[:, :], src_w[dc * 128:(dc + 1) * 128, :], s_wkl[b])
                p.wait("vector", s_wkl[b], t_l)
                p.em("vector", lambda v, i=stage, o=dstb[:, dc * 512:(dc + 1) * 512],
                     s=s_wkc.h: v.tensor_copy(out=o, in_=i[:, :]).then_inc(s, 1))
                s_wkc.inc(1)

        s_rl = S("a_rl"); s_kmm = S("a_kmm"); s_kro = S("a_kro")
        p.wait("tensor", s_wkc, 16)
        p.wait("vector", s_wkc, 32)
        for ec in range(NKV):
            for tt in range(4):
                j = ec * 4 + tt
                bb = j % 2
                ps = psq[bb]
                p.wait("sync", s_kro, j)
                dma("sync", cos_sb[:, :], cos_in[:, tt * 512:(tt + 1) * 512], s_rl)
                t_rl = dma("sync", sin_sb[:, :], sin_in[:, tt * 512:(tt + 1) * 512], s_rl)
                p.wait("tensor", s_kro, j - 1)
                for dc in range(16):
                    p.em("tensor", lambda e, o=ps, l=wk_sb[:, dc * 512 + ec * 128: dc * 512 + (ec + 1) * 128],
                         r=big_buf[:, dc * 2048 + tt * 512: dc * 2048 + (tt + 1) * 512],
                         st=(dc == 0), sp=(dc == 15), s=s_kmm.h:
                         (lambda i: (i.then_inc(s, 1) if sp else None, i)[1])(e.matmul(o[:, :], l, r, start=st, stop=sp)))
                t_mm = s_kmm.inc(1)
                p.wait("vector", s_kmm, t_mm)
                p.wait("vector", s_rl, t_rl)

                def ropef(v, ps=ps, dst=kfm[:, ec * 2048 + tt * 512: ec * 2048 + (tt + 1) * 512]):
                    v.drain()
                    v.tensor_copy(out=kraw[:, :], in_=ps[:, :])
                    v.drain()
                    v.tensor_scalar(out=krot[0:64, :], in0=kraw[64:128, :], scalar1=-1.0, scalar2=None, op0=OP.mult)
                    v.tensor_copy(out=krot[64:128, :], in_=kraw[0:64, :])
                    v.drain()
                    v.tensor_tensor(out=kraw[:, :], in0=kraw[:, :], in1=cos_sb[:, :], op=OP.mult)
                    v.drain()
                    v.tensor_tensor(out=krot[:, :], in0=krot[:, :], in1=sin_sb[:, :], op=OP.mult)
                    v.drain()
                    return v.tensor_tensor(out=dst, in0=kraw[:, :], in1=krot[:, :], op=OP.add)
                p.em("vector", lambda v, f=ropef, s=s_kro.h: f(v).then_inc(s, 1))
                s_kro.inc(1)

        s_vmm = S("a_vmm"); s_vev = S("a_vev")
        p.wait("tensor", s_kro, 16)
        for ts in range(16):
            bb = ts % 2
            ps = psq[bb]
            p.wait("tensor", s_vev, ts - 1)
            for dc in range(16):
                p.em("tensor", lambda e, o=ps, l=big_buf[:, dc * 2048 + ts * 128: dc * 2048 + (ts + 1) * 128],
                     r=wv_sb[:, dc * 512:(dc + 1) * 512],
                     st=(dc == 0), sp=(dc == 15), s=s_vmm.h:
                     (lambda i: (i.then_inc(s, 1) if sp else None, i)[1])(e.matmul(o[:, :], l, r, start=st, stop=sp)))
            t_mm = s_vmm.inc(1)
            p.wait("scalar", s_vmm, t_mm)
            p.em("scalar", lambda e, i=ps, o=vtm[:, ts * 512:(ts + 1) * 512], s=s_vev.h:
                 e.copy(out=o, in_=i[:, :]).then_inc(s, 1))
            s_vev.inc(1)

        s_ktp = S("a_ktp"); s_kte = S("a_kte")
        for ec in range(NKV):
            for ch in range(NCH):
                j = ec * NCH + ch
                ps = pstp[j % 2]
                p.wait("tensor", s_kte, j - 1)
                p.em("tensor", lambda e, i=kfm[:, ec * 2048 + ch * 64: ec * 2048 + (ch + 1) * 64],
                     o=ps, s=s_ktp.h: e.transpose(o[0:64, :], i, id_sb[:, :]).then_inc(s, 1))
                t_tp = s_ktp.inc(1)
                p.wait("scalar", s_ktp, t_tp)
                p.em("scalar", lambda e, i=ps,
                     o=ktm[(ch % 2) * 64:(ch % 2) * 64 + 64, ec * 2048 + (ch // 2) * 128: ec * 2048 + (ch // 2) * 128 + 128],
                     s=s_kte.h: e.copy(out=o, in_=i[0:64, :]).then_inc(s, 1))
                s_kte.inc(1)

        s_ktd = S("a_ktd"); s_amm = S("a_amm"); s_aup = S("a_aup")
        p.em("vector", lambda v: (v.memset(kv_sb[:, :], 0.0), v.drain())[0])
        p.wait("vector", s_cl, t_cl)
        p.wait("vector", s_kte, 4 * NCH)
        for ch in range(NCH):
            for h in range(NH):
                j = ch * NH + h
                kvh = h // 4
                rb = j % 4
                ktv = ktm[(ch % 2) * 64:(ch % 2) * 64 + 64,
                          kvh * 2048 + (ch // 2) * 128: kvh * 2048 + (ch // 2) * 128 + 128]
                p.wait("vector", s_amm, j - 3)
                p.em("vector", lambda v, i=ktv, o=ktd[(ch % 2) * 64:(ch % 2) * 64 + 64, rb * 128:(rb + 1) * 128],
                     sc=kdc[(ch % 2) * 64:(ch % 2) * 64 + 64, h:h + 1], s=s_ktd.h:
                     v.tensor_scalar(out=o, in0=i, scalar1=sc, scalar2=None, op0=OP.mult).then_inc(s, 1))
                t_ktd = s_ktd.inc(1)
                ps = pskv[j % 2]
                p.wait("tensor", s_ktd, t_ktd)
                p.wait("tensor", s_aup, j - 1)
                p.em("tensor", lambda e, o=ps,
                     l=ktd[(ch % 2) * 64:(ch % 2) * 64 + 64, rb * 128:(rb + 1) * 128],
                     r=vtm[(ch % 2) * 64:(ch % 2) * 64 + 64, (ch // 2) * 512 + kvh * 128: (ch // 2) * 512 + (kvh + 1) * 128],
                     s=s_amm.h: e.matmul(o[:, :], l, r, start=True, stop=True).then_inc(s, 1))
                t_mm = s_amm.inc(1)
                p.wait("vector", s_amm, t_mm)
                p.em("vector", lambda v, ps=ps, a=kv_sb[:, h * 128:(h + 1) * 128], bl=BLK[h], s=s_aup.h:
                     v.scalar_tensor_tensor(out=a, in0=a, scalar=bl, in1=ps[:, :], op0=OP.mult, op1=OP.add).then_inc(s, 1))
                s_aup.inc(1)

        s_kvw = S("a_kvw"); s_kvg = S("a_kvg"); s_kvr = S("a_kvr"); s_kvi = S("a_kvi")
        p.wait("gpsimd", s_aup, s_aup.v)
        for h in range(NH):
            dma("gpsimd", kvloc_d[h * 128:(h + 1) * 128, :], kv_sb[:, h * 128:(h + 1) * 128], s_kvw)
        p.wait("gpsimd", s_kvw, s_kvw.v)
        p.em("gpsimd", lambda e, s=s_kvg.h:
             e.collective_compute("AllGather", OP.bypass,
                                  replica_groups=[list(range(NCORES))],
                                  ins=[kvloc_d], outs=[kvall_d]).then_inc(s, 1))
        s_kvg.inc(1)
        p.wait("vector", s_kvw, s_kvw.v)
        p.em("vector", lambda v: (v.memset(kv_sb[:, :], 0.0), v.drain())[0])
        p.wait("sync", s_kvg, 1)
        for r in range(NCORES):
            src_r = kvall_d[r * 2048:(r + 1) * 2048, :].rearrange("(h p) e -> p h e", p=128)
            dst_r = kvr[:, :].rearrange("p (h e) -> p h e", h=NH)
            p.wait("sync", s_kvi, NH * r)
            t_r = dma("sync", dst_r, src_r, s_kvr)
            p.wait("vector", s_kvr, t_r)
            for h in range(NH):
                p.em("vector", lambda v, i=kvr[:, h * 128:(h + 1) * 128],
                     a=kv_sb[:, h * 128:(h + 1) * 128], sc=cw_sb[:, r * NH + h:r * NH + h + 1],
                     s=s_kvi.h: v.scalar_tensor_tensor(out=a, in0=i, scalar=sc, in1=a, op0=OP.mult, op1=OP.add).then_inc(s, 1))
                s_kvi.inc(1)

        apstprel()
        aps2, aps2rel = ps_scope([
            ("ap_s0", [64, 64], F32), ("ap_s1", [64, 64], F32),
            ("ap_o0", [128, 64], F32), ("ap_o1", [128, 64], F32),
        ])
        pss = [aps2["ap_s0"], aps2["ap_s1"]]
        pso = [aps2["ap_o0"], aps2["ap_o1"]]

        s_wql = S("a_wql"); s_wol = S("a_wol"); s_y2l = S("a_y2l"); s_qmm = S("a_qmm")
        s_qro = S("a_qro"); s_sev = S("a_sev"); s_qdv = S("a_qdv"); s_kbf = S("a_kbf")
        s_bmm = S("a_bmm"); s_oev = S("a_oev"); s_bup = S("a_bup"); s_bktd = S("a_bktd")
        s_opm = S("a_opm"); s_x1l = S("a_x1l"); s_x2w = S("a_x2w"); s_x2v = S("a_x2v")
        for tb in range(4):
            if tb == 0:
                p.wait("sync", s_kmm, 16)
                p.wait("sync", s_vmm, 16)
            p.wait("sync", s_opm, 16 * tb)
            for dc in range(16):
                dma("sync", big_buf[:, dc * 2048:(dc + 1) * 2048], wq_bf[dc * 128:(dc + 1) * 128, :], s_wql)
            t_wq = s_wql.v
            if tb == 0:
                p.wait("sync", s_kmm, 16)
            p.wait("sync", s_qmm, s_qmm.v)
            for dc in range(16):
                t_y2 = dma("sync", y2b[:, dc * 512:(dc + 1) * 512],
                           y2_d[dc * 128:(dc + 1) * 128, tb * 512:(tb + 1) * 512], s_y2l)
            p.wait("sync", s_qro, 16 * tb)
            dma("sync", cos_sb[:, :], cos_in[:, tb * 512:(tb + 1) * 512], s_rl)
            t_rl2 = dma("sync", sin_sb[:, :], sin_in[:, tb * 512:(tb + 1) * 512], s_rl)
            p.wait("vector", s_bmm, 384 * tb)
            for hc in range(NH):
                jj = tb * NH + hc
                bb = jj % 2
                ps = psq[bb]
                if hc == 0:
                    p.wait("tensor", s_wql, t_wq)
                    p.wait("tensor", s_y2l, t_y2)
                    p.wait("tensor", s_x2v, 16 * tb)
                p.wait("tensor", s_qro, jj - 1)
                for dc in range(16):
                    p.em("tensor", lambda e, o=ps, l=big_buf[:, dc * 2048 + hc * 128: dc * 2048 + (hc + 1) * 128],
                         r=y2b[:, dc * 512:(dc + 1) * 512],
                         st=(dc == 0), sp=(dc == 15), s=s_qmm.h:
                         (lambda i: (i.then_inc(s, 1) if sp else None, i)[1])(e.matmul(o[:, :], l, r, start=st, stop=sp)))
                t_mm = s_qmm.inc(1)
                p.wait("vector", s_qmm, t_mm)
                p.wait("vector", s_rl, t_rl2)

                def qropef(v, ps=ps, dst=q_blk[:, hc * 512:(hc + 1) * 512]):
                    v.drain()
                    v.tensor_copy(out=kraw[:, :], in_=ps[:, :])
                    v.drain()
                    v.tensor_scalar(out=krot[0:64, :], in0=kraw[64:128, :], scalar1=-1.0, scalar2=None, op0=OP.mult)
                    v.tensor_copy(out=krot[64:128, :], in_=kraw[0:64, :])
                    v.drain()
                    v.tensor_tensor(out=kraw[:, :], in0=kraw[:, :], in1=cos_sb[:, :], op=OP.mult)
                    v.drain()
                    v.tensor_tensor(out=krot[:, :], in0=krot[:, :], in1=sin_sb[:, :], op=OP.mult)
                    v.drain()
                    return v.tensor_tensor(out=dst, in0=kraw[:, :], in1=krot[:, :], op=OP.add)
                p.em("vector", lambda v, f=qropef, s=s_qro.h: f(v).then_inc(s, 1))
                s_qro.inc(1)
            p.wait("sync", s_qmm, s_qmm.v)
            for dc in range(16):
                dma("sync", big_buf[:, dc * 2048:(dc + 1) * 2048], wo_bf[dc * 128:(dc + 1) * 128, :], s_wol)
            t_wo = s_wol.v
            p.wait("tensor", s_qro, 16 * (tb + 1))
            for ch8 in range(8):
                gc = tb * 8 + ch8
                for h in range(NH):
                    j = gc * NH + h
                    kvh = h // 4
                    b2 = j % 2
                    rb = j % 4
                    if gc >= 1:
                        p.wait("vector", s_bmm, 3 * ((gc - 1) * NH + h) + 2)
                    if ch8 == 0 and h == 0:
                        p.wait("scalar", s_opm, 16 * tb)
                        if tb == 0:
                            p.wait("scalar", s_vev, 16)
                    p.em("vector", lambda v, i=kv_sb[:, h * 128:(h + 1) * 128],
                         o=kv_bf[:, h * 128:(h + 1) * 128], s=s_kbf.h:
                         v.tensor_copy(out=o, in_=i).then_inc(s, 1))
                    t_kbf = s_kbf.inc(1)
                    p.wait("vector", s_bmm, 3 * (j - 4) + 2)
                    p.em("vector", lambda v, i=q_blk[:, h * 512 + ch8 * 64: h * 512 + (ch8 + 1) * 64],
                         o=qdb[:, rb * 64:(rb + 1) * 64], m=qdc[:, h * 64:(h + 1) * 64],
                         s=s_qdv.h: v.tensor_tensor(out=o, in0=i, in1=m, op=OP.mult).then_inc(s, 1))
                    t_qd = s_qdv.inc(1)
                    ps_s = pss[b2]
                    p.wait("tensor", s_sev, j - 1)
                    p.em("tensor", lambda e, o=ps_s,
                         l=kfm[:, kvh * 2048 + gc * 64: kvh * 2048 + (gc + 1) * 64],
                         r=q_blk[:, h * 512 + ch8 * 64: h * 512 + (ch8 + 1) * 64],
                         s=s_bmm.h: e.matmul(o[:, :], l, r, start=True, stop=True).then_inc(s, 1))
                    t_smm = s_bmm.inc(1)
                    p.wait("vector", s_bmm, t_smm)
                    p.em("vector", lambda v, i=ps_s, m=msk[:, h * 64:(h + 1) * 64],
                         o=scb[(gc % 2) * 64:(gc % 2) * 64 + 64, rb * 64:(rb + 1) * 64], s=s_sev.h:
                         v.tensor_tensor(out=o, in0=i[:, :], in1=m, op=OP.mult).then_inc(s, 1))
                    t_sev = s_sev.inc(1)
                    ps_o = pso[b2]
                    p.wait("tensor", s_sev, t_sev)
                    p.wait("tensor", s_qdv, t_qd)
                    p.wait("tensor", s_kbf, t_kbf)
                    p.wait("tensor", s_oev, j - 1)
                    p.em("tensor", lambda e, o=ps_o,
                         l=vtm[(gc % 2) * 64:(gc % 2) * 64 + 64, (gc // 2) * 512 + kvh * 128: (gc // 2) * 512 + (kvh + 1) * 128],
                         r=scb[(gc % 2) * 64:(gc % 2) * 64 + 64, rb * 64:(rb + 1) * 64]:
                         e.matmul(o[:, :], l, r, start=True, stop=False))
                    p.em("tensor", lambda e, o=ps_o, l=kv_bf[:, h * 128:(h + 1) * 128],
                         r=qdb[:, rb * 64:(rb + 1) * 64], s=s_bmm.h:
                         e.matmul(o[:, :], l, r, start=False, stop=True).then_inc(s, 1))
                    t_omm = s_bmm.inc(1)
                    p.wait("scalar", s_bmm, t_omm)
                    p.em("scalar", lambda e, i=ps_o, o=o_blk[:, h * 512 + ch8 * 64: h * 512 + (ch8 + 1) * 64],
                         s=s_oev.h: e.copy(out=o, in_=i[:, :]).then_inc(s, 1))
                    s_oev.inc(1)
                    ktv = ktm[(gc % 2) * 64:(gc % 2) * 64 + 64,
                              kvh * 2048 + (gc // 2) * 128: kvh * 2048 + (gc // 2) * 128 + 128]
                    p.wait("vector", s_bmm, t_omm)
                    p.em("vector", lambda v, i=ktv, o=ktd[(gc % 2) * 64:(gc % 2) * 64 + 64, rb * 128:(rb + 1) * 128],
                         sc=kdc[(gc % 2) * 64:(gc % 2) * 64 + 64, h:h + 1], s=s_bktd.h:
                         v.tensor_scalar(out=o, in0=i, scalar1=sc, scalar2=None, op0=OP.mult).then_inc(s, 1))
                    t_ktd = s_bktd.inc(1)
                    ps_k = pskv[b2]
                    p.wait("tensor", s_bktd, t_ktd)
                    p.wait("tensor", s_bup, j - 1)
                    p.em("tensor", lambda e, o=ps_k,
                         l=ktd[(gc % 2) * 64:(gc % 2) * 64 + 64, rb * 128:(rb + 1) * 128],
                         r=vtm[(gc % 2) * 64:(gc % 2) * 64 + 64, (gc // 2) * 512 + kvh * 128: (gc // 2) * 512 + (kvh + 1) * 128],
                         s=s_bmm.h: e.matmul(o[:, :], l, r, start=True, stop=True).then_inc(s, 1))
                    t_kmm = s_bmm.inc(1)
                    p.wait("vector", s_bmm, t_kmm)
                    p.em("vector", lambda v, ps=ps_k, a=kv_sb[:, h * 128:(h + 1) * 128], bl=BLK[h], s=s_bup.h:
                         v.scalar_tensor_tensor(out=a, in0=a, scalar=bl, in1=ps[:, :], op0=OP.mult, op1=OP.add).then_inc(s, 1))
                    s_bup.inc(1)
            for ts in range(4):
                row = (tb * 4 + ts) * 128
                p.wait("sync", s_x2v, s_x2v.v)
                t_x1 = dma("sync", x1sb[:, :], x1_d[row:row + 128, :], s_x1l)
                p.wait("vector", s_x2w, s_x2w.v)
                for dd in range(4):
                    jj = (tb * 4 + ts) * 4 + dd
                    bb = jj % 2
                    ps = psq[bb]
                    if ts == 0 and dd == 0:
                        p.wait("tensor", s_wol, t_wo)
                        p.wait("tensor", s_oev, (tb + 1) * 128)
                    p.wait("tensor", s_x2v, jj - 1)
                    for ec in range(16):
                        p.em("tensor", lambda e, o=ps, l=o_blk[:, ec * 512 + ts * 128: ec * 512 + (ts + 1) * 128],
                             r=big_buf[:, ec * 2048 + dd * 512: ec * 2048 + (dd + 1) * 512],
                             st=(ec == 0), sp=(ec == 15), s=s_opm.h:
                             (lambda i: (i.then_inc(s, 1) if sp else None, i)[1])(e.matmul(o[:, :], l, r, start=st, stop=sp)))
                    t_om = s_opm.inc(1)
                    p.wait("vector", s_opm, t_om)
                    p.wait("vector", s_x1l, t_x1)
                    p.em("vector", lambda v, ps=ps, xi=x1sb[:, dd * 512:(dd + 1) * 512],
                         xo=x2sb[:, dd * 512:(dd + 1) * 512], s=s_x2v.h:
                         v.scalar_tensor_tensor(out=xo, in0=ps[:, :], scalar=0.0, in1=xi, op0=OP.bypass, op1=OP.add).then_inc(s, 1))
                    s_x2v.inc(1)
                p.wait("gpsimd", s_x2v, s_x2v.v)
                dma("gpsimd", x2_d[row:row + 128, :], x2sb[:, :], s_x2w)
        p.wait("gpsimd", s_x2w, s_x2w.v)
        p.barrier(bar)
        aps2rel()
        aps1rel()
        arel()

    if DO_LN1:
        emit_ln("ln1", x_in, mirror=False)
        if DEBUG and "y" in dbg:
            s_dby = S("s_dby")
            for dc in range(16):
                t = dma("sync", dbg["y"][dc * 128:(dc + 1) * 128, :], big_buf[:, dc * 2048:(dc + 1) * 2048], s_dby)
            p.wait("sync", s_dby, t)
    if DO_GU1:
        emit_gateup("g1")
        if DEBUG and "inter" in dbg:
            s_dbi = S("s_dbi")
            t = dma("sync", dbg["inter"], inter_d[0:2048, :], s_dbi)
            p.wait("sync", s_dbi, t)
    if DO_DN1:
        s_x1st = emit_down("d1", x_in, x1_d)
        if DEBUG:
            s_db1 = S("s_db1")
            p.wait("sync", s_x1st, s_x1st.v)
            t = dma("sync", dbg["x1"], x1_d, s_db1)
            p.wait("sync", s_db1, t)

    if STAGE >= 3:
        emit_ln("ln2", x1_d, mirror=True)
        emit_attention()
        if DEBUG:
            s_db2 = S("s_db2")
            t = dma("sync", dbg["x2"], x2_d, s_db2)
            p.wait("sync", s_db2, t)

    if STAGE >= 4:
        emit_ln("ln3", x2_d, mirror=False)
        emit_gateup("g2")
        s_ost = emit_down("d2", x2_d, out)
        p.wait("sync", s_ost, s_ost.v)

    p.barrier(bar)

    with nc.Block() as block:
        @block.sync
        def _(e):
            for f in p.q["sync"]:
                f(e)

        @block.vector
        def _(e):
            for f in p.q["vector"]:
                f(e)

        @block.scalar
        def _(e):
            for f in p.q["scalar"]:
                f(e)

        @block.tensor
        def _(e):
            for f in p.q["tensor"]:
                f(e)

        @block.gpsimd
        def _(e):
            for f in p.q["gpsimd"]:
                f(e)

    es.close()
    return nc


_lam = 2.0 ** (-8.0 * (np.arange(1, NH + 1)) / NH)
BLK = [float(np.exp(-_lam[h] * CHUNK)) for h in range(NH)]



def _host_constants(core):
    bf = ml_dtypes.bfloat16
    lam = _lam.astype(np.float64)
    pos0 = (core % 4) * T
    pos = np.arange(pos0, pos0 + T, dtype=np.float64)
    inv = 1.0 / 10000.0 ** (np.arange(0, DH, 2, dtype=np.float64) / DH)
    fr = np.concatenate([inv, inv])[:, None] * pos[None, :]
    cos_t = np.cos(fr).astype(np.float32)
    sin_t = np.sin(fr).astype(np.float32)
    r = np.arange(CHUNK, dtype=np.float64)
    diff = r[:, None] - r[None, :]
    maskT = np.zeros((CHUNK, NH * CHUNK), np.float32)
    for h in range(NH):
        m = np.where(diff >= 0, np.exp(-lam[h] * diff), 0.0)
        maskT[:, h * CHUNK:(h + 1) * CHUNK] = m.T
    qdec = np.zeros((128, NH * CHUNK), np.float32)
    for h in range(NH):
        qdec[:, h * CHUNK:(h + 1) * CHUNK] = np.exp(-lam[h] * (r + 1.0))[None, :]
    kdec = np.zeros((128, NH), np.float32)
    for h in range(NH):
        kdec[0:64, h] = np.exp(-lam[h] * (CHUNK - 1.0 - r))
        kdec[64:128, h] = kdec[0:64, h]
    blk = np.exp(-lam * CHUNK)
    cw = np.zeros((128, NCORES * NH), np.float32)
    g0 = (core // 4) * 4
    for rnk in range(g0, core):
        w = blk ** (32.0 * (core - rnk - 1))
        cw[:, rnk * NH:(rnk + 1) * NH] = w[None, :].astype(np.float32)
    return {
        "ident": np.eye(128).astype(bf),
        "cos_t": cos_t, "sin_t": sin_t,
        "maskT": maskT.astype(bf), "qdec_t": qdec.astype(bf),
        "kdec_t": kdec, "cw_t": cw,
    }


def kernel(L0_attn_x, L0_attn_wq, L0_attn_wk, L0_attn_wv, L0_attn_wo,
           L0_ffn_gate_W_packed, L0_ffn_gate_scales, L0_ffn_up_W_packed,
           L0_ffn_up_scales, L0_ffn_down_W_packed, L0_ffn_down_scales):
    x = np.ascontiguousarray(np.asarray(L0_attn_x, dtype=np.float32).reshape(2 * 8192, D))
    wqT = np.ascontiguousarray(np.asarray(L0_attn_wq).T)
    wkT = np.ascontiguousarray(np.asarray(L0_attn_wk).T)
    wvT = np.ascontiguousarray(np.asarray(L0_attn_wv).T)
    woT = np.ascontiguousarray(np.asarray(L0_attn_wo).T)
    pg = np.asarray(L0_ffn_gate_W_packed, dtype=np.int32)
    pu = np.asarray(L0_ffn_up_W_packed, dtype=np.int32)
    pd = np.asarray(L0_ffn_down_W_packed, dtype=np.int32)
    sg = np.asarray(L0_ffn_gate_scales, dtype=np.float32)
    su = np.asarray(L0_ffn_up_scales, dtype=np.float32)
    sd = np.asarray(L0_ffn_down_scales, dtype=np.float32)

    nc = build_nc()
    in_maps = []
    for c in range(NCORES):
        m = {
            "x_in": x[c * T:(c + 1) * T],
            "wqT": wqT, "wkT": wkT, "wvT": wvT, "woT": woT,
            "pk_g": np.ascontiguousarray(pg[c * T:(c + 1) * T]),
            "pk_u": np.ascontiguousarray(pu[c * T:(c + 1) * T]),
            "pk_d": np.ascontiguousarray(pd[:, c * 1024:(c + 1) * 1024]),
            "sc_g": np.ascontiguousarray(sg[c * T:(c + 1) * T]),
            "sc_u": np.ascontiguousarray(su[c * T:(c + 1) * T]),
            "sc_d": np.ascontiguousarray(sd[:, c * 64:(c + 1) * 64]),
        }
        m.update(_host_constants(c))
        in_maps.append(m)
    res = run_bass_kernel_spmd(nc, in_maps, core_ids=list(range(NCORES)))
    outs = [res.results[c]["out"] for c in range(NCORES)]
    full = np.concatenate(outs, 0).reshape(2, 8192, D).astype(np.float32)
    if DEBUG:
        kernel._dbg = res.results
    return full


# version 02 (final)
# speedup vs baseline: 1.0251x; 1.0251x over previous
import os
import numpy as np
import ml_dtypes
import concourse.bass as bass
import concourse.mybir as mybir
from concourse.bass_utils import run_bass_kernel_spmd

F32 = mybir.dt.float32
BF16 = mybir.dt.bfloat16
I32 = mybir.dt.int32
OP = mybir.AluOpType
AF = mybir.ActivationFunctionType
AX = mybir.AxisListType

NCORES = 8
T = 2048
D = 2048
NMLP = 16384
NPAN = 128
NPAN_LOC = 16
CHUNK = 64
NCH = 32
NH, NKV, DH = 16, 4, 128
GELU_C = 0.044715
GELU_S = 2.0 * 0.7978845608028654
EPS = 1e-5

_S = os.environ.get("KSTAGE", "4")
DO_DEC = _S in ("1", "gu", "2", "3", "4")
DO_LN1 = _S in ("ln", "gu", "2", "3", "4")
DO_GU1 = _S in ("gu", "2", "3", "4")
DO_DN1 = _S in ("2", "3", "4")
STAGE = {"1": 1, "ln": 1, "gu": 1, "2": 2, "3": 3, "4": 4}[_S]
DEBUG = os.environ.get("KDEBUG", "") == "1"

ENGS = ("sync", "vector", "scalar", "tensor", "gpsimd")


class Sem:
    __slots__ = ("h", "v", "name")

    def __init__(self, es, nc, name):
        self.h = es.enter_context(nc.semaphore(name))
        self.v = 0
        self.name = name

    def inc(self, n=1):
        self.v += n
        return self.v


class Prog:
    def __init__(self):
        self.q = {k: [] for k in ENGS}

    def em(self, eng, fn):
        self.q[eng].append(fn)

    def wait(self, eng, sem, val=None):
        v = sem.v if val is None else val
        if v <= 0:
            return
        self.q[eng].append(lambda e, s=sem.h, t=v: e.wait_ge(s, t))

    def barrier(self, sems):
        for eng in ENGS:
            s = sems[eng]
            s.inc(1)
            if eng == "sync":
                self.q[eng].append(lambda e, sh=s.h: e.nop().then_inc(sh, 1))
            else:
                self.q[eng].append(lambda e, sh=s.h: e.drain().then_inc(sh, 1))
        for eng in ENGS:
            for o in ENGS:
                if o != eng:
                    self.wait(eng, sems[o])


def build_nc():
    nc = bass.Bass("TRN2", target_bir_lowering=False)
    from contextlib import ExitStack
    es = ExitStack()
    p = Prog()

    def din(name, shape, dt):
        return nc.dram_tensor(name, shape, dt, kind="ExternalInput").ap()

    x_in = din("x_in", [T, D], F32)
    wqT = din("wqT", [D, NH * DH], F32)
    wkT = din("wkT", [D, NKV * DH], F32)
    wvT = din("wvT", [D, NKV * DH], F32)
    woT = din("woT", [NH * DH, D], F32)
    pk_g = din("pk_g", [T, 1024], I32)
    pk_u = din("pk_u", [T, 1024], I32)
    pk_d = din("pk_d", [D, 1024], I32)
    sc_g = din("sc_g", [T, 64], F32)
    sc_u = din("sc_u", [T, 64], F32)
    sc_d = din("sc_d", [D, 64], F32)
    ident = din("ident", [128, 128], BF16)
    cos_in = din("cos_t", [DH, T], F32)
    sin_in = din("sin_t", [DH, T], F32)
    mask_in = din("maskT", [CHUNK, NH * CHUNK], BF16)
    qdec_in = din("qdec_t", [128, NH * CHUNK], BF16)
    kdec_in = din("kdec_t", [128, NH], F32)
    cw_in = din("cw_t", [128, NCORES * NH], F32)

    out = nc.dram_tensor("out", [T, D], F32, kind="ExternalOutput").ap()

    dbg = {}
    if DEBUG:
        if DO_LN1:
            dbg["y"] = nc.dram_tensor("dbg_y", [16 * 128, D], BF16, kind="ExternalOutput").ap()
        if DO_GU1:
            dbg["inter"] = nc.dram_tensor("dbg_inter", [2048, T], BF16, kind="ExternalOutput").ap()
        if DO_DN1:
            dbg["x1"] = nc.dram_tensor("dbg_x1", [T, D], F32, kind="ExternalOutput").ap()
        if STAGE >= 3:
            dbg["x2"] = nc.dram_tensor("dbg_x2", [T, D], F32, kind="ExternalOutput").ap()

    def dint(name, shape, dt):
        return nc.dram_tensor(name, shape, dt, kind="Internal").ap()

    cl_g = dint("cl_g", [NPAN_LOC * 128, D], BF16)
    cl_u = dint("cl_u", [NPAN_LOC * 128, D], BF16)
    cl_d = dint("cl_d", [NPAN_LOC * 128, D], BF16)
    cf_g = nc.dram_tensor("cf_g", [NPAN * 128, D], BF16, kind="Internal", addr_space="Shared").ap()
    cf_u = nc.dram_tensor("cf_u", [NPAN * 128, D], BF16, kind="Internal", addr_space="Shared").ap()
    cf_d = nc.dram_tensor("cf_d", [NPAN * 128, D], BF16, kind="Internal", addr_space="Shared").ap()
    inter_d = dint("inter_d", [NMLP, T], BF16)
    x1_d = dint("x1_d", [T, D], F32)
    x2_d = dint("x2_d", [T, D], F32)
    y2_d = dint("y2_d", [16 * 128, T], BF16)
    wq_bf = dint("wq_bf", [D, NH * DH], BF16)
    wo_bf = dint("wo_bf", [NH * DH, D], BF16)
    kvloc_d = dint("kvloc_d", [NH * DH, DH], F32)
    kvall_d = dint("kvall_d", [NCORES * NH * DH, DH], F32)

    sems = {}

    def S(name):
        if name in sems:
            return sems[name]
        s = Sem(es, nc, name)
        sems[name] = s
        return s

    bar = {eng: S("bar_" + eng) for eng in ENGS}

    def sb_scope(specs):
        guards = {}
        ts = {}
        for name, shape, dt in specs:
            g = nc.sbuf_tensor(name, shape, dt)
            ts[name] = g.__enter__()
            guards[name] = g
        def rel():
            for g in reversed(list(guards.values())):
                g.__exit__(None, None, None)
        return ts, rel

    def ps_scope(specs):
        guards = {}
        ts = {}
        for name, shape, dt in specs:
            g = nc.psum_tensor(name, shape, dt)
            ts[name] = g.__enter__()
            guards[name] = g
        def rel():
            for g in reversed(list(guards.values())):
                g.__exit__(None, None, None)
        return ts, rel

    def dma(eng, out_ap, in_ap, sem):
        p.em(eng, lambda e, o=out_ap, i=in_ap, s=sem.h: e.dma_start(out=o, in_=i).then_inc(s, 16))
        return sem.inc(16)

    big_buf = es.enter_context(nc.sbuf_tensor("big_buf", [128, 16 * 2048], BF16))
    id_sb = es.enter_context(nc.sbuf_tensor("id_sb", [128, 128], BF16))

    s_const = S("s_const")
    dma("sync", id_sb[:, :], ident, s_const)

    dec_specs = [("pk_sb0", [128, 1024], I32), ("pk_sb1", [128, 1024], I32),
                 ("sc_sb0", [128, 64], F32), ("sc_sb1", [128, 64], F32),
                 ("dec_out0", [128, 2048], BF16), ("dec_out1", [128, 2048], BF16),
                 ("pan_buf0", [128, 2048], BF16), ("pan_buf1", [128, 2048], BF16)]
    dec_tmp_names = ["lo_i", "wf", "xv", "bv", "mv", "e2", "e4", "e6", "zv", "uv", "tv", "fac", "pw", "val", "sgn"]
    for k in dec_tmp_names:
        dec_specs.append(("dx_" + k, [128, 1024], I32 if k == "lo_i" else F32))
    dsb, dec_rel = sb_scope(dec_specs)
    dps, dec_ps_rel = ps_scope([(f"ps_tp{i}", [128, 128], BF16) for i in range(4)])
    ps_tp = [dps[f"ps_tp{i}"] for i in range(4)]
    pk_sb = [dsb["pk_sb0"], dsb["pk_sb1"]]
    sc_sb = [dsb["sc_sb0"], dsb["sc_sb1"]]
    dec_out = [dsb["dec_out0"], dsb["dec_out1"]]
    pan_buf = [dsb["pan_buf0"], dsb["pan_buf1"]]

    s_pk = [S("s_pk0"), S("s_pk1")]
    s_dec = S("s_dec"); s_tp = S("s_tp"); s_tpe = S("s_tpe")
    s_cw = [S("s_cw0"), S("s_cw1")]
    s_gat_g = S("s_gat_g"); s_gat_u = S("s_gat_u"); s_gat_d = S("s_gat_d")

    def emit_decode_tile(b, fold_half):
        def f(v):
            pi = pk_sb[b]
            lo_i = dsb["dx_lo_i"]; wf = dsb["dx_wf"]; x = dsb["dx_xv"]
            bv = dsb["dx_bv"]; m = dsb["dx_mv"]
            e2 = dsb["dx_e2"]; e4 = dsb["dx_e4"]; e6 = dsb["dx_e6"]; z = dsb["dx_zv"]
            u = dsb["dx_uv"]; t = dsb["dx_tv"]; fac = dsb["dx_fac"]
            pw = dsb["dx_pw"]; val = dsb["dx_val"]; sgn = dsb["dx_sgn"]
            ot = dec_out[b]; st = sc_sb[b]
            v.tensor_scalar(out=lo_i[:], in0=pi[:], scalar1=15, scalar2=None, op0=OP.bitwise_and)
            v.tensor_copy(out=wf[:], in_=pi[:])
            v.drain()
            ins = None
            for which, off in ((0, 0), (1, 1)):
                if which == 0:
                    v.tensor_copy(out=x[:], in_=lo_i[:])
                    s = 1.0
                else:
                    v.drain()
                    v.scalar_tensor_tensor(out=x[:], in0=x[:], scalar=-1.0, in1=wf[:], op0=OP.mult, op1=OP.add)
                    s = 16.0
                v.drain()
                v.tensor_scalar(out=bv[:], in0=x[:], scalar1=7.5 * s, scalar2=None, op0=OP.is_gt)
                v.drain()
                v.scalar_tensor_tensor(out=m[:], in0=bv[:], scalar=-8.0 * s, in1=x[:], op0=OP.mult, op1=OP.add)
                v.drain()
                v.tensor_scalar(out=e2[:], in0=m[:], scalar1=1.5 * s, scalar2=None, op0=OP.is_gt)
                v.tensor_scalar(out=e4[:], in0=m[:], scalar1=3.5 * s, scalar2=None, op0=OP.is_gt)
                v.tensor_scalar(out=e6[:], in0=m[:], scalar1=5.5 * s, scalar2=None, op0=OP.is_gt)
                v.tensor_scalar(out=z[:], in0=m[:], scalar1=0.5 * s, scalar2=None, op0=OP.is_gt)
                v.tensor_scalar(out=t[:], in0=m[:], scalar1=0.5 / s, scalar2=1.0, op0=OP.mult, op1=OP.add)
                v.drain()
                v.tensor_tensor(out=u[:], in0=e2[:], in1=e4[:], op=OP.add)
                v.drain()
                v.tensor_tensor(out=u[:], in0=u[:], in1=e6[:], op=OP.add)
                v.drain()
                v.tensor_tensor(out=fac[:], in0=t[:], in1=u[:], op=OP.subtract)
                v.drain()
                v.scalar_tensor_tensor(out=fac[:], in0=z[:], scalar=-0.5, in1=fac[:], op0=OP.mult, op1=OP.add)
                v.drain()
                v.scalar_tensor_tensor(out=fac[:], in0=e2[:], scalar=0.5, in1=fac[:], op0=OP.mult, op1=OP.add)
                v.scalar_tensor_tensor(out=pw[:], in0=e2[:], scalar=0.5, in1=e4[:], op0=OP.mult, op1=OP.add)
                v.drain()
                v.scalar_tensor_tensor(out=pw[:], in0=e6[:], scalar=2.0, in1=pw[:], op0=OP.mult, op1=OP.add)
                v.drain()
                v.tensor_scalar(out=pw[:], in0=pw[:], scalar1=0.5, scalar2=None, op0=OP.add)
                v.tensor_scalar(out=sgn[:], in0=bv[:], scalar1=-2.0, scalar2=1.0, op0=OP.mult, op1=OP.add)
                v.drain()
                v.tensor_tensor(out=val[:], in0=pw[:], in1=fac[:], op=OP.mult)
                v.drain()
                v.tensor_tensor(out=val[:], in0=val[:], in1=z[:], op=OP.mult)
                v.drain()
                v.tensor_tensor(out=val[:], in0=val[:], in1=sgn[:], op=OP.mult)
                v.drain()
                st_rep = st[:, :].unsqueeze(2).broadcast_to([128, 64, 16])
                val3 = val[:, :].rearrange("p (a c) -> p a c", a=64)
                ot_v = ot[:, off::2].rearrange("p (a c) -> p a c", a=64)
                if fold_half:
                    ins = v.scalar_tensor_tensor(out=ot_v, in0=val3, scalar=0.5, in1=st_rep, op0=OP.mult, op1=OP.mult)
                else:
                    ins = v.tensor_tensor(out=ot_v, in0=val3, in1=st_rep, op=OP.mult)
            return ins
        return f

    def emit_decode_phase():
        jobs = [(pk_g, sc_g, cl_g, cf_g, False), (pk_u, sc_u, cl_u, cf_u, False),
                (pk_d, sc_d, cl_d, cf_d, True)]
        p.wait("tensor", s_const, 16)
        for wi, (pk, sc, cl, cf, fold) in enumerate(jobs):
            is_wd = (wi == 2)
            for rt in range(NPAN_LOC):
                j = wi * NPAN_LOC + rt
                b = j % 2
                p.wait("sync", s_dec, j - 1)
                dma("sync", pk_sb[b][:, :], pk[rt * 128:(rt + 1) * 128, :], s_pk[b])
                t_sc = dma("sync", sc_sb[b][:, :], sc[rt * 128:(rt + 1) * 128, :], s_pk[b])
                p.wait("vector", s_pk[b], t_sc)
                p.wait("vector", s_tp, (j - 1) * 16)
                fdec = emit_decode_tile(b, fold)
                p.em("vector", lambda v, f=fdec, s=s_dec.h: f(v).then_inc(s, 1))
                t_dec = s_dec.inc(1)
                p.wait("tensor", s_dec, t_dec)
                for dc in range(16):
                    ktp = s_tp.v
                    pb = ktp % 4
                    p.wait("tensor", s_tpe, ktp - 3)
                    p.em("tensor", lambda e, i=dec_out[b][:, dc * 128:(dc + 1) * 128], o=ps_tp[pb],
                         s=s_tp.h: e.transpose(o[:, :], i, id_sb[:, :]).then_inc(s, 1))
                    t_tp = s_tp.inc(1)
                    if is_wd:
                        dst = big_buf[:, dc * 2048 + rt * 128: dc * 2048 + rt * 128 + 128]
                    else:
                        dst = pan_buf[b][:, dc * 128:(dc + 1) * 128]
                        if dc == 0:
                            p.wait("scalar", s_cw[b], s_cw[b].v)
                    p.wait("scalar", s_tp, t_tp)
                    p.em("scalar", lambda e, i=ps_tp[pb], o=dst,
                         s=s_tpe.h: e.copy(out=o, in_=i[:, :]).then_inc(s, 1))
                    s_tpe.inc(1)
                if not is_wd:
                    p.wait("gpsimd", s_tpe, s_tpe.v)
                    dma("gpsimd", cl[rt * 128:(rt + 1) * 128, :], pan_buf[b][:, :], s_cw[b])
                    if rt % 4 == 3:
                        sg = [s_gat_g, s_gat_u][wi]
                        sub = rt // 4
                        p.wait("gpsimd", s_cw[0], s_cw[0].v)
                        p.wait("gpsimd", s_cw[1], s_cw[1].v)
                        p.em("gpsimd", lambda e, i=cl[sub * 512:(sub + 1) * 512, :],
                             o=cf[sub * 8 * 512:(sub + 1) * 8 * 512, :], s=sg.h:
                             e.collective_compute("AllGather", OP.bypass,
                                                  replica_groups=[list(range(NCORES))],
                                                  ins=[i], outs=[o]).then_inc(s, 1))
                        sg.inc(1)
            if is_wd:
                p.wait("gpsimd", s_tpe, s_tpe.v)
                for ns in range(NPAN_LOC):
                    dma("gpsimd", cl[ns * 128:(ns + 1) * 128, :],
                        big_buf[:, ns * 2048:(ns + 1) * 2048], s_cw[ns % 2])
                p.wait("gpsimd", s_cw[0], s_cw[0].v)
                p.wait("gpsimd", s_cw[1], s_cw[1].v)
                for sub in range(4):
                    p.em("gpsimd", lambda e, i=cl[sub * 512:(sub + 1) * 512, :],
                         o=cf[sub * 8 * 512:(sub + 1) * 8 * 512, :], s=s_gat_d.h:
                         e.collective_compute("AllGather", OP.bypass,
                                              replica_groups=[list(range(NCORES))],
                                              ins=[i], outs=[o]).then_inc(s, 1))
                    s_gat_d.inc(1)

    if DO_DEC:
        emit_decode_phase()
    p.barrier(bar)
    dec_ps_rel()
    dec_rel()

    if STAGE >= 3:
        wsb, wrel = sb_scope([("wc_f0", [128, 2048], F32), ("wc_f1", [128, 2048], F32),
                              ("wc_b0", [128, 2048], BF16), ("wc_b1", [128, 2048], BF16)])
        s_wcl = [S("s_wcl0"), S("s_wcl1")]
        s_wcc = S("s_wcc"); s_wcs = [S("s_wcs0"), S("s_wcs1")]
        for wi, (src, dst) in enumerate(((wqT, wq_bf), (woT, wo_bf))):
            for rt in range(16):
                j = wi * 16 + rt
                b = j % 2
                p.wait("sync", s_wcc, j - 1)
                t_l = dma("sync", wsb[f"wc_f{b}"][:, :], src[rt * 128:(rt + 1) * 128, :], s_wcl[b])
                p.wait("vector", s_wcl[b], t_l)
                p.wait("vector", s_wcs[b], s_wcs[b].v)
                p.em("vector", lambda v, i=wsb[f"wc_f{b}"], o=wsb[f"wc_b{b}"], s=s_wcc.h:
                     v.tensor_copy(out=o[:, :], in_=i[:, :]).then_inc(s, 1))
                s_wcc.inc(1)
                p.wait("gpsimd", s_wcc, s_wcc.v)
                dma("gpsimd", dst[rt * 128:(rt + 1) * 128, :], wsb[f"wc_b{b}"][:, :], s_wcs[b])
        p.wait("gpsimd", s_wcs[0], s_wcs[0].v)
        p.wait("gpsimd", s_wcs[1], s_wcs[1].v)
        p.barrier(bar)
        wrel()

    def emit_ln(tag, src_dram, mirror):
        lsb, lrel = sb_scope([
            (f"{tag}_xs0", [128, 2048], F32), (f"{tag}_xs1", [128, 2048], F32),
            (f"{tag}_xc", [128, 2048], F32), (f"{tag}_sq", [128, 2048], F32),
            (f"{tag}_st", [128, 8], F32),
            (f"{tag}_yt0", [128, 2048], BF16), (f"{tag}_yt1", [128, 2048], BF16),
        ])
        lps, lpsrel = ps_scope([(f"{tag}_pst{i}", [128, 128], BF16) for i in range(4)])
        pst = [lps[f"{tag}_pst{i}"] for i in range(4)]
        s_lx = [S("ln_lx0"), S("ln_lx1")]
        s_ly = S("ln_ly"); s_lsq = S("ln_lsq")
        s_lrs = S("ln_lrs"); s_ltp = S("ln_ltp"); s_lte = S("ln_lte")
        s_lm = S("ln_lm")
        b_ly = s_ly.v; b_ltp = s_ltp.v
        xc, sq, st = lsb[f"{tag}_xc"], lsb[f"{tag}_sq"], lsb[f"{tag}_st"]
        for ts in range(16):
            b = ts % 2
            xs = lsb[f"{tag}_xs{b}"]
            yt = lsb[f"{tag}_yt{b}"]
            p.wait("sync", s_ly, b_ly + ts - 1)
            t_x = dma("sync", xs[:, :], src_dram[ts * 128:(ts + 1) * 128, :], s_lx[b])
            p.wait("vector", s_lx[b], t_x)

            def ln_chain(v, xs=xs, yt=yt, xc=xc, sq=sq, st=st):
                v.tensor_reduce(out=st[:, 0:1], in_=xs[:, :], axis=AX.X, op=OP.add)
                v.drain()
                v.tensor_scalar(out=st[:, 1:2], in0=st[:, 0:1], scalar1=1.0 / D, scalar2=None, op0=OP.mult)
                v.drain()
                return v.tensor_scalar(out=xc[:, :], in0=xs[:, :], scalar1=st[:, 1:2], scalar2=None, op0=OP.subtract)
            p.em("vector", lambda v, f=ln_chain, s=s_lsq.h: f(v).then_inc(s, 1))
            t_sq = s_lsq.inc(1)
            p.wait("scalar", s_lsq, t_sq)
            p.em("scalar", lambda e, s=s_lsq.h: e.activation(out=sq[:, :], in_=xc[:, :], func=AF.Square).then_inc(s, 1))
            t_sq2 = s_lsq.inc(1)
            p.wait("vector", s_lsq, t_sq2)

            def var_chain(v, sq=sq, st=st):
                v.tensor_reduce(out=st[:, 2:3], in_=sq[:, :], axis=AX.X, op=OP.add)
                v.drain()
                return v.tensor_scalar(out=st[:, 3:4], in0=st[:, 2:3], scalar1=1.0 / D, scalar2=EPS, op0=OP.mult, op1=OP.add)
            p.em("vector", lambda v, f=var_chain, s=s_lrs.h: f(v).then_inc(s, 1))
            t_v = s_lrs.inc(1)
            p.wait("scalar", s_lrs, t_v)
            p.em("scalar", lambda e, s=s_lrs.h: e.activation(out=st[:, 4:5], in_=st[:, 3:4], func=AF.Sqrt).then_inc(s, 1))
            t_s = s_lrs.inc(1)
            p.wait("vector", s_lrs, t_s)

            p.wait("vector", s_ltp, b_ltp + 16 * (ts - 1))

            def y_chain(v, yt=yt, xc=xc, st=st):
                v.reciprocal(out=st[:, 5:6], in_=st[:, 4:5])
                v.drain()
                return v.tensor_scalar(out=yt[:, :], in0=xc[:, :], scalar1=st[:, 5:6], scalar2=None, op0=OP.mult)
            p.em("vector", lambda v, f=y_chain, s=s_ly.h: f(v).then_inc(s, 1))
            t_y = s_ly.inc(1)
            p.wait("tensor", s_ly, t_y)
            for dc in range(16):
                ktp = s_ltp.v
                pb = ktp % 4
                p.wait("tensor", s_lte, ktp - 3)
                p.em("tensor", lambda e, i=yt[:, dc * 128:(dc + 1) * 128], o=pst[pb],
                     s=s_ltp.h: e.transpose(o[:, :], i, id_sb[:, :]).then_inc(s, 1))
                t_tp = s_ltp.inc(1)
                p.wait("scalar", s_ltp, t_tp)
                p.em("scalar", lambda e, i=pst[pb],
                     o=big_buf[:, dc * 2048 + ts * 128: dc * 2048 + ts * 128 + 128],
                     s=s_lte.h: e.copy(out=o, in_=i[:, :]).then_inc(s, 1))
                s_lte.inc(1)
        if mirror:
            p.wait("gpsimd", s_lte, s_lte.v)
            for dc in range(16):
                dma("gpsimd", y2_d[dc * 128:(dc + 1) * 128, :], big_buf[:, dc * 2048:(dc + 1) * 2048], s_lm)
            p.wait("gpsimd", s_lm, s_lm.v)
        p.barrier(bar)
        lpsrel()
        lrel()

    def emit_gateup(tag):
        gsb, grel = sb_scope([
            (f"{tag}_wg0", [128, 2048], BF16), (f"{tag}_wg1", [128, 2048], BF16),
            (f"{tag}_wu0", [128, 2048], BF16), (f"{tag}_wu1", [128, 2048], BF16),
            (f"{tag}_sq0", [128, 512], F32), (f"{tag}_sq1", [128, 512], F32),
            (f"{tag}_g30", [128, 512], F32), (f"{tag}_g31", [128, 512], F32),
            (f"{tag}_z00", [128, 512], F32), (f"{tag}_z01", [128, 512], F32),
            (f"{tag}_t10", [128, 512], F32), (f"{tag}_t11", [128, 512], F32),
            (f"{tag}_w10", [128, 512], F32), (f"{tag}_w11", [128, 512], F32),
            (f"{tag}_it0", [128, 512], BF16), (f"{tag}_it1", [128, 512], BF16),
        ])
        gps, gpsrel = ps_scope([(f"{tag}_pg0", [128, 512], F32), (f"{tag}_pg1", [128, 512], F32),
                                (f"{tag}_pu0", [128, 512], F32), (f"{tag}_pu1", [128, 512], F32)])
        pg = [gps[f"{tag}_pg0"], gps[f"{tag}_pg1"]]
        pu = [gps[f"{tag}_pu0"], gps[f"{tag}_pu1"]]
        s_wl = [S("gu_wl0"), S("gu_wl1")]
        s_mm = S("gu_mm"); s_sq = S("gu_sq")
        s_z0 = S("gu_z0"); s_t1 = S("gu_t1"); s_ge = S("gu_ge")
        s_iw = [S("gu_iw0"), S("gu_iw1")]
        b_mm = s_mm.v; b_ge = s_ge.v; b_z0 = s_z0.v; b_t1 = s_t1.v
        for pan in range(NPAN):
            b = pan % 2
            wg = gsb[f"{tag}_wg{b}"]; wu = gsb[f"{tag}_wu{b}"]
            p.wait("sync", s_mm, b_mm + (pan - 1) * 8)
            p.wait("sync", s_gat_g, pan // 32 + 1)
            p.wait("sync", s_gat_u, pan // 32 + 1)
            dma("sync", wg[:, :], cf_g[pan * 128:(pan + 1) * 128, :], s_wl[b])
            t_w = dma("sync", wu[:, :], cf_u[pan * 128:(pan + 1) * 128, :], s_wl[b])
            for tt in range(4):
                k = pan * 4 + tt
                bb = k % 2
                psg = pg[bb]; psu = pu[bb]
                if tt == 0:
                    p.wait("tensor", s_wl[b], t_w)
                p.wait("tensor", s_ge, b_ge + k - 1)
                for dc in range(16):
                    def mmg(e, o=psg, l=wg[:, dc * 128:(dc + 1) * 128],
                            r=big_buf[:, dc * 2048 + tt * 512: dc * 2048 + (tt + 1) * 512],
                            st=(dc == 0), sp=(dc == 15), s=s_mm.h):
                        ins = e.matmul(o[:, :], l, r, start=st, stop=sp)
                        if sp:
                            ins.then_inc(s, 1)
                        return ins
                    p.em("tensor", mmg)
                s_mm.inc(1)
                for dc in range(16):
                    def mmu(e, o=psu, l=wu[:, dc * 128:(dc + 1) * 128],
                            r=big_buf[:, dc * 2048 + tt * 512: dc * 2048 + (tt + 1) * 512],
                            st=(dc == 0), sp=(dc == 15), s=s_mm.h):
                        ins = e.matmul(o[:, :], l, r, start=st, stop=sp)
                        if sp:
                            ins.then_inc(s, 1)
                        return ins
                    p.em("tensor", mmu)
                t_mmu = s_mm.inc(1)
                sqb = gsb[f"{tag}_sq{bb}"]; g3 = gsb[f"{tag}_g3{bb}"]
                z0 = gsb[f"{tag}_z0{bb}"]; t1 = gsb[f"{tag}_t1{bb}"]
                w1 = gsb[f"{tag}_w1{bb}"]; itb = gsb[f"{tag}_it{bb}"]
                p.wait("scalar", s_mm, t_mmu - 1)
                p.wait("scalar", s_z0, b_z0 + k - 1)
                p.em("scalar", lambda e, o=sqb, i=psg, s=s_sq.h:
                     e.activation(out=o[:, :], in_=i[:, :], func=AF.Square).then_inc(s, 1))
                t_sq = s_sq.inc(1)
                p.wait("vector", s_sq, t_sq)
                p.wait("vector", s_t1, b_t1 + k - 1)

                def zf(v, sqb=sqb, g3=g3, z0=z0, psg=psg):
                    v.tensor_tensor(out=g3[:, :], in0=sqb[:, :], in1=psg[:, :], op=OP.mult)
                    v.drain()
                    return v.scalar_tensor_tensor(out=z0[:, :], in0=g3[:, :], scalar=GELU_C, in1=psg[:, :], op0=OP.mult, op1=OP.add)
                p.em("vector", lambda v, f=zf, s=s_z0.h: f(v).then_inc(s, 1))
                t_z0 = s_z0.inc(1)
                p.wait("scalar", s_z0, t_z0)
                p.em("scalar", lambda e, o=t1, i=z0, s=s_t1.h:
                     e.activation(out=o[:, :], in_=i[:, :], func=AF.Sigmoid, scale=GELU_S).then_inc(s, 1))
                t_t1 = s_t1.inc(1)
                p.wait("vector", s_t1, t_t1)
                p.wait("vector", s_mm, t_mmu)
                p.wait("vector", s_iw[bb], s_iw[bb].v)

                def gf(v, t1=t1, w1=w1, itb=itb, psg=psg, psu=psu):
                    v.tensor_tensor(out=w1[:, :], in0=t1[:, :], in1=psg[:, :], op=OP.mult)
                    v.drain()
                    return v.tensor_tensor(out=itb[:, :], in0=w1[:, :], in1=psu[:, :], op=OP.mult)
                p.em("vector", lambda v, f=gf, s=s_ge.h: f(v).then_inc(s, 1))
                s_ge.inc(1)
                p.wait("gpsimd", s_ge, s_ge.v)
                dma("gpsimd", inter_d[pan * 128:(pan + 1) * 128, tt * 512:(tt + 1) * 512], itb[:, :], s_iw[bb])
        p.wait("gpsimd", s_iw[0], s_iw[0].v)
        p.wait("gpsimd", s_iw[1], s_iw[1].v)
        p.barrier(bar)
        gpsrel()
        grel()

    def emit_down(tag, src_res, dst):
        dnsb, dnrel = sb_scope([
            (f"{tag}_acc", [128, 8 * 2048], F32),
            (f"{tag}_it", [128, 16 * 1024], BF16),
        ])
        dn_ps, dn_psrel = ps_scope([(f"{tag}_pa", [128, 2048], F32), (f"{tag}_pb", [128, 2048], F32)])
        psd = [dn_ps[f"{tag}_pa"], dn_ps[f"{tag}_pb"]]
        acc = dnsb[f"{tag}_acc"]; itb = dnsb[f"{tag}_it"]
        s_ai = S("dn_ai")
        s_ld = [S("dn_ld0"), S("dn_ld1")]
        s_fl = S("dn_fl"); s_st = S("dn_st"); s_mmdn = S("dn_mmdn")
        b_st = s_st.v; b_mmdn = s_mmdn.v; b_fl = s_fl.v
        p.wait("sync", s_gat_d, 4)
        for th in range(2):
            p.wait("sync", s_st, b_st + 16 * 8 * th)
            for ts in range(8):
                dma("sync", acc[:, ts * 2048:(ts + 1) * 2048],
                    src_res[(th * 8 + ts) * 128:(th * 8 + ts + 1) * 128, :], s_ai)
            t_ai = s_ai.v
            for sub in range(16):
                gsub = th * 16 + sub
                sl0 = (sub % 2) * 8
                p.wait("sync", s_mmdn, b_mmdn + 8 * (gsub - 1))
                for pp in range(8):
                    pan = sub * 8 + pp
                    dma("sync", big_buf[:, (sl0 + pp) * 2048:(sl0 + pp + 1) * 2048],
                        cf_d[pan * 128:(pan + 1) * 128, :], s_ld[sub % 2])
                    t_ld = dma("sync", itb[:, (sl0 + pp) * 1024:(sl0 + pp + 1) * 1024],
                               inter_d[pan * 128:(pan + 1) * 128, th * 1024:(th + 1) * 1024], s_ld[sub % 2])
                p.wait("tensor", s_ld[sub % 2], t_ld)
                for tb in range(8):
                    kt = gsub * 8 + tb
                    bb = tb % 2
                    ps = psd[bb]
                    p.wait("tensor", s_fl, b_fl + kt - 1)
                    for pp in range(8):
                        for hh in range(4):
                            def mmd(e, o=ps[:, hh * 512:(hh + 1) * 512],
                                    l=itb[:, (sl0 + pp) * 1024 + tb * 128: (sl0 + pp) * 1024 + (tb + 1) * 128],
                                    r=big_buf[:, (sl0 + pp) * 2048 + hh * 512: (sl0 + pp) * 2048 + (hh + 1) * 512],
                                    st=(pp == 0), sp=(pp == 7), fin=(pp == 7 and hh == 3), s=s_mmdn.h):
                                ins = e.matmul(o, l, r, start=st, stop=sp)
                                if fin:
                                    ins.then_inc(s, 1)
                                return ins
                            p.em("tensor", mmd)
                    t_mm = s_mmdn.inc(1)
                    p.wait("vector", s_mmdn, t_mm)
                    if sub == 0 and tb == 0:
                        p.wait("vector", s_ai, t_ai)

                    def ff(v, ps=ps, a=acc[:, tb * 2048:(tb + 1) * 2048]):
                        return v.scalar_tensor_tensor(out=a, in0=ps[:, :], scalar=0.0, in1=a, op0=OP.bypass, op1=OP.add)
                    p.em("vector", lambda v, f=ff, s=s_fl.h: f(v).then_inc(s, 1))
                    s_fl.inc(1)
            p.wait("gpsimd", s_fl, s_fl.v)
            for ts in range(8):
                dma("gpsimd", dst[(th * 8 + ts) * 128:(th * 8 + ts + 1) * 128, :],
                    acc[:, ts * 2048:(ts + 1) * 2048], s_st)
        p.wait("gpsimd", s_st, s_st.v)
        p.barrier(bar)
        dn_psrel()
        dnrel()
        return s_st

    def emit_attention():
        asb, arel = sb_scope([
            ("at_wk", [128, 16 * 512], BF16), ("at_wv", [128, 16 * 512], BF16),
            ("at_kfm", [128, 4 * 2048], BF16), ("at_ktm", [128, 4 * 2048], BF16),
            ("at_vtm", [128, 16 * 512], BF16),
            ("at_kraw", [128, 512], F32), ("at_krot", [128, 512], F32),
            ("at_cos", [128, 512], F32), ("at_sin", [128, 512], F32),
            ("at_kv", [128, 16 * 128], F32), ("at_kvb", [128, 16 * 128], BF16),
            ("at_qb", [128, 16 * 512], BF16),
            ("at_ktd", [128, 4 * 128], BF16),
            ("at_sc", [128, 4 * 64], BF16), ("at_qd", [128, 4 * 64], BF16),
            ("at_x1", [128, 2048], F32), ("at_x2", [128, 2048], F32),
            ("at_msk", [64, NH * 64], BF16), ("at_qdc", [128, NH * 64], BF16),
            ("at_kdc", [128, NH], F32), ("at_cw", [128, NCORES * NH], F32),
        ])
        aps1, aps1rel = ps_scope([
            ("ap_q0", [128, 512], F32), ("ap_q1", [128, 512], F32),
            ("ap_kv0", [128, 128], F32), ("ap_kv1", [128, 128], F32),
        ])
        aps_tp, apstprel = ps_scope([("ap_tp0", [128, 128], BF16), ("ap_tp1", [128, 128], BF16)])
        wk_sb, wv_sb = asb["at_wk"], asb["at_wv"]
        kfm, ktm, vtm = asb["at_kfm"], asb["at_ktm"], asb["at_vtm"]
        kraw, krot = asb["at_kraw"], asb["at_krot"]
        cos_sb, sin_sb = asb["at_cos"], asb["at_sin"]
        kv_sb, kv_bf = asb["at_kv"], asb["at_kvb"]
        q_blk = asb["at_qb"]
        y2b = asb["at_wk"]
        o_blk = asb["at_wv"]
        ktd = asb["at_ktd"]; scb = asb["at_sc"]; qdb = asb["at_qd"]
        x1sb, x2sb = asb["at_x1"], asb["at_x2"]
        kvr = x1sb
        msk, qdc, kdc, cw_sb = asb["at_msk"], asb["at_qdc"], asb["at_kdc"], asb["at_cw"]
        psq = [aps1["ap_q0"], aps1["ap_q1"]]
        pskv = [aps1["ap_kv0"], aps1["ap_kv1"]]
        pstp = [aps_tp["ap_tp0"], aps_tp["ap_tp1"]]

        s_cl = S("a_cl")
        dma("sync", msk[:, :], mask_in, s_cl)
        dma("sync", qdc[:, :], qdec_in, s_cl)
        dma("sync", kdc[:, :], kdec_in, s_cl)
        t_cl = dma("sync", cw_sb[:, :], cw_in, s_cl)

        s_wkl = [S("a_wkl0"), S("a_wkl1")]
        s_wkc = S("a_wkc")
        for wi, (src_w, dstb) in enumerate(((wkT, wk_sb), (wvT, wv_sb))):
            for dc in range(16):
                j = wi * 16 + dc
                b = j % 2
                stage = kraw if b == 0 else krot
                p.wait("sync", s_wkc, j - 1)
                t_l = dma("sync", stage[:, :], src_w[dc * 128:(dc + 1) * 128, :], s_wkl[b])
                p.wait("vector", s_wkl[b], t_l)
                p.em("vector", lambda v, i=stage, o=dstb[:, dc * 512:(dc + 1) * 512],
                     s=s_wkc.h: v.tensor_copy(out=o, in_=i[:, :]).then_inc(s, 1))
                s_wkc.inc(1)

        s_rl = S("a_rl"); s_kmm = S("a_kmm"); s_kro = S("a_kro")
        p.wait("tensor", s_wkc, 16)
        p.wait("vector", s_wkc, 32)
        for ec in range(NKV):
            for tt in range(4):
                j = ec * 4 + tt
                bb = j % 2
                ps = psq[bb]
                p.wait("sync", s_kro, j)
                dma("sync", cos_sb[:, :], cos_in[:, tt * 512:(tt + 1) * 512], s_rl)
                t_rl = dma("sync", sin_sb[:, :], sin_in[:, tt * 512:(tt + 1) * 512], s_rl)
                p.wait("tensor", s_kro, j - 1)
                for dc in range(16):
                    p.em("tensor", lambda e, o=ps, l=wk_sb[:, dc * 512 + ec * 128: dc * 512 + (ec + 1) * 128],
                         r=big_buf[:, dc * 2048 + tt * 512: dc * 2048 + (tt + 1) * 512],
                         st=(dc == 0), sp=(dc == 15), s=s_kmm.h:
                         (lambda i: (i.then_inc(s, 1) if sp else None, i)[1])(e.matmul(o[:, :], l, r, start=st, stop=sp)))
                t_mm = s_kmm.inc(1)
                p.wait("vector", s_kmm, t_mm)
                p.wait("vector", s_rl, t_rl)

                def ropef(v, ps=ps, dst=kfm[:, ec * 2048 + tt * 512: ec * 2048 + (tt + 1) * 512]):
                    v.drain()
                    v.tensor_copy(out=kraw[:, :], in_=ps[:, :])
                    v.drain()
                    v.tensor_scalar(out=krot[0:64, :], in0=kraw[64:128, :], scalar1=-1.0, scalar2=None, op0=OP.mult)
                    v.tensor_copy(out=krot[64:128, :], in_=kraw[0:64, :])
                    v.drain()
                    v.tensor_tensor(out=kraw[:, :], in0=kraw[:, :], in1=cos_sb[:, :], op=OP.mult)
                    v.drain()
                    v.tensor_tensor(out=krot[:, :], in0=krot[:, :], in1=sin_sb[:, :], op=OP.mult)
                    v.drain()
                    return v.tensor_tensor(out=dst, in0=kraw[:, :], in1=krot[:, :], op=OP.add)
                p.em("vector", lambda v, f=ropef, s=s_kro.h: f(v).then_inc(s, 1))
                s_kro.inc(1)

        s_vmm = S("a_vmm"); s_vev = S("a_vev")
        p.wait("tensor", s_kro, 16)
        for ts in range(16):
            bb = ts % 2
            ps = psq[bb]
            p.wait("tensor", s_vev, ts - 1)
            for dc in range(16):
                p.em("tensor", lambda e, o=ps, l=big_buf[:, dc * 2048 + ts * 128: dc * 2048 + (ts + 1) * 128],
                     r=wv_sb[:, dc * 512:(dc + 1) * 512],
                     st=(dc == 0), sp=(dc == 15), s=s_vmm.h:
                     (lambda i: (i.then_inc(s, 1) if sp else None, i)[1])(e.matmul(o[:, :], l, r, start=st, stop=sp)))
            t_mm = s_vmm.inc(1)
            p.wait("scalar", s_vmm, t_mm)
            p.em("scalar", lambda e, i=ps, o=vtm[:, ts * 512:(ts + 1) * 512], s=s_vev.h:
                 e.copy(out=o, in_=i[:, :]).then_inc(s, 1))
            s_vev.inc(1)

        s_ktp = S("a_ktp"); s_kte = S("a_kte")
        for ec in range(NKV):
            for ch in range(NCH):
                j = ec * NCH + ch
                ps = pstp[j % 2]
                p.wait("tensor", s_kte, j - 1)
                p.em("tensor", lambda e, i=kfm[:, ec * 2048 + ch * 64: ec * 2048 + (ch + 1) * 64],
                     o=ps, s=s_ktp.h: e.transpose(o[0:64, :], i, id_sb[:, :]).then_inc(s, 1))
                t_tp = s_ktp.inc(1)
                p.wait("scalar", s_ktp, t_tp)
                p.em("scalar", lambda e, i=ps,
                     o=ktm[(ch % 2) * 64:(ch % 2) * 64 + 64, ec * 2048 + (ch // 2) * 128: ec * 2048 + (ch // 2) * 128 + 128],
                     s=s_kte.h: e.copy(out=o, in_=i[0:64, :]).then_inc(s, 1))
                s_kte.inc(1)

        s_ktd = S("a_ktd"); s_amm = S("a_amm"); s_aup = S("a_aup")
        p.em("vector", lambda v: (v.memset(kv_sb[:, :], 0.0), v.drain())[0])
        p.wait("vector", s_cl, t_cl)
        p.wait("vector", s_kte, 4 * NCH)
        for ch in range(NCH):
            for h in range(NH):
                j = ch * NH + h
                kvh = h // 4
                rb = j % 4
                ktv = ktm[(ch % 2) * 64:(ch % 2) * 64 + 64,
                          kvh * 2048 + (ch // 2) * 128: kvh * 2048 + (ch // 2) * 128 + 128]
                p.wait("vector", s_amm, j - 3)
                p.em("vector", lambda v, i=ktv, o=ktd[(ch % 2) * 64:(ch % 2) * 64 + 64, rb * 128:(rb + 1) * 128],
                     sc=kdc[(ch % 2) * 64:(ch % 2) * 64 + 64, h:h + 1], s=s_ktd.h:
                     v.tensor_scalar(out=o, in0=i, scalar1=sc, scalar2=None, op0=OP.mult).then_inc(s, 1))
                t_ktd = s_ktd.inc(1)
                ps = pskv[j % 2]
                p.wait("tensor", s_ktd, t_ktd)
                p.wait("tensor", s_aup, j - 1)
                p.em("tensor", lambda e, o=ps,
                     l=ktd[(ch % 2) * 64:(ch % 2) * 64 + 64, rb * 128:(rb + 1) * 128],
                     r=vtm[(ch % 2) * 64:(ch % 2) * 64 + 64, (ch // 2) * 512 + kvh * 128: (ch // 2) * 512 + (kvh + 1) * 128],
                     s=s_amm.h: e.matmul(o[:, :], l, r, start=True, stop=True).then_inc(s, 1))
                t_mm = s_amm.inc(1)
                p.wait("vector", s_amm, t_mm)
                p.em("vector", lambda v, ps=ps, a=kv_sb[:, h * 128:(h + 1) * 128], bl=BLK[h], s=s_aup.h:
                     v.scalar_tensor_tensor(out=a, in0=a, scalar=bl, in1=ps[:, :], op0=OP.mult, op1=OP.add).then_inc(s, 1))
                s_aup.inc(1)

        s_kvw = S("a_kvw"); s_kvg = S("a_kvg"); s_kvr = S("a_kvr"); s_kvi = S("a_kvi")
        p.wait("gpsimd", s_aup, s_aup.v)
        for h in range(NH):
            dma("gpsimd", kvloc_d[h * 128:(h + 1) * 128, :], kv_sb[:, h * 128:(h + 1) * 128], s_kvw)
        p.wait("gpsimd", s_kvw, s_kvw.v)
        p.em("gpsimd", lambda e, s=s_kvg.h:
             e.collective_compute("AllGather", OP.bypass,
                                  replica_groups=[list(range(NCORES))],
                                  ins=[kvloc_d], outs=[kvall_d]).then_inc(s, 1))
        s_kvg.inc(1)
        p.wait("vector", s_kvw, s_kvw.v)
        p.em("vector", lambda v: (v.memset(kv_sb[:, :], 0.0), v.drain())[0])
        p.wait("sync", s_kvg, 1)
        for r in range(NCORES):
            src_r = kvall_d[r * 2048:(r + 1) * 2048, :].rearrange("(h p) e -> p h e", p=128)
            dst_r = kvr[:, :].rearrange("p (h e) -> p h e", h=NH)
            p.wait("sync", s_kvi, NH * r)
            t_r = dma("sync", dst_r, src_r, s_kvr)
            p.wait("vector", s_kvr, t_r)
            for h in range(NH):
                p.em("vector", lambda v, i=kvr[:, h * 128:(h + 1) * 128],
                     a=kv_sb[:, h * 128:(h + 1) * 128], sc=cw_sb[:, r * NH + h:r * NH + h + 1],
                     s=s_kvi.h: v.scalar_tensor_tensor(out=a, in0=i, scalar=sc, in1=a, op0=OP.mult, op1=OP.add).then_inc(s, 1))
                s_kvi.inc(1)

        apstprel()
        aps2, aps2rel = ps_scope([
            ("ap_s0", [64, 64], F32), ("ap_s1", [64, 64], F32),
            ("ap_o0", [128, 64], F32), ("ap_o1", [128, 64], F32),
        ])
        pss = [aps2["ap_s0"], aps2["ap_s1"]]
        pso = [aps2["ap_o0"], aps2["ap_o1"]]

        s_wql = S("a_wql"); s_wol = S("a_wol"); s_y2l = S("a_y2l"); s_qmm = S("a_qmm")
        s_qro = S("a_qro"); s_sev = S("a_sev"); s_qdv = S("a_qdv"); s_kbf = S("a_kbf")
        s_bmm = S("a_bmm"); s_oev = S("a_oev"); s_bup = S("a_bup"); s_bktd = S("a_bktd")
        s_opm = S("a_opm"); s_x1l = S("a_x1l"); s_x2w = S("a_x2w"); s_x2v = S("a_x2v")
        for tb in range(4):
            if tb == 0:
                p.wait("sync", s_kmm, 16)
                p.wait("sync", s_vmm, 16)
            p.wait("sync", s_opm, 16 * tb)
            for dc in range(16):
                dma("sync", big_buf[:, dc * 2048:(dc + 1) * 2048], wq_bf[dc * 128:(dc + 1) * 128, :], s_wql)
            t_wq = s_wql.v
            if tb == 0:
                p.wait("sync", s_kmm, 16)
            p.wait("sync", s_qmm, s_qmm.v)
            for dc in range(16):
                t_y2 = dma("sync", y2b[:, dc * 512:(dc + 1) * 512],
                           y2_d[dc * 128:(dc + 1) * 128, tb * 512:(tb + 1) * 512], s_y2l)
            p.wait("sync", s_qro, 16 * tb)
            dma("sync", cos_sb[:, :], cos_in[:, tb * 512:(tb + 1) * 512], s_rl)
            t_rl2 = dma("sync", sin_sb[:, :], sin_in[:, tb * 512:(tb + 1) * 512], s_rl)
            p.wait("vector", s_bmm, 384 * tb)
            for hc in range(NH):
                jj = tb * NH + hc
                bb = jj % 2
                ps = psq[bb]
                if hc == 0:
                    p.wait("tensor", s_wql, t_wq)
                    p.wait("tensor", s_y2l, t_y2)
                    p.wait("tensor", s_x2v, 16 * tb)
                p.wait("tensor", s_qro, jj - 1)
                for dc in range(16):
                    p.em("tensor", lambda e, o=ps, l=big_buf[:, dc * 2048 + hc * 128: dc * 2048 + (hc + 1) * 128],
                         r=y2b[:, dc * 512:(dc + 1) * 512],
                         st=(dc == 0), sp=(dc == 15), s=s_qmm.h:
                         (lambda i: (i.then_inc(s, 1) if sp else None, i)[1])(e.matmul(o[:, :], l, r, start=st, stop=sp)))
                t_mm = s_qmm.inc(1)
                p.wait("vector", s_qmm, t_mm)
                p.wait("vector", s_rl, t_rl2)

                def qropef(v, ps=ps, dst=q_blk[:, hc * 512:(hc + 1) * 512]):
                    v.drain()
                    v.tensor_copy(out=kraw[:, :], in_=ps[:, :])
                    v.drain()
                    v.tensor_scalar(out=krot[0:64, :], in0=kraw[64:128, :], scalar1=-1.0, scalar2=None, op0=OP.mult)
                    v.tensor_copy(out=krot[64:128, :], in_=kraw[0:64, :])
                    v.drain()
                    v.tensor_tensor(out=kraw[:, :], in0=kraw[:, :], in1=cos_sb[:, :], op=OP.mult)
                    v.drain()
                    v.tensor_tensor(out=krot[:, :], in0=krot[:, :], in1=sin_sb[:, :], op=OP.mult)
                    v.drain()
                    return v.tensor_tensor(out=dst, in0=kraw[:, :], in1=krot[:, :], op=OP.add)
                p.em("vector", lambda v, f=qropef, s=s_qro.h: f(v).then_inc(s, 1))
                s_qro.inc(1)
            p.wait("sync", s_qmm, s_qmm.v)
            for dc in range(16):
                dma("sync", big_buf[:, dc * 2048:(dc + 1) * 2048], wo_bf[dc * 128:(dc + 1) * 128, :], s_wol)
            t_wo = s_wol.v
            p.wait("tensor", s_qro, 16 * (tb + 1))
            for ch8 in range(8):
                gc = tb * 8 + ch8
                for h in range(NH):
                    j = gc * NH + h
                    kvh = h // 4
                    b2 = j % 2
                    rb = j % 4
                    if gc >= 1:
                        p.wait("vector", s_bmm, 3 * ((gc - 1) * NH + h) + 2)
                    if ch8 == 0 and h == 0:
                        p.wait("scalar", s_opm, 16 * tb)
                        if tb == 0:
                            p.wait("scalar", s_vev, 16)
                    p.em("vector", lambda v, i=kv_sb[:, h * 128:(h + 1) * 128],
                         o=kv_bf[:, h * 128:(h + 1) * 128], s=s_kbf.h:
                         v.tensor_copy(out=o, in_=i).then_inc(s, 1))
                    t_kbf = s_kbf.inc(1)
                    p.wait("vector", s_bmm, 3 * (j - 4) + 2)
                    p.em("vector", lambda v, i=q_blk[:, h * 512 + ch8 * 64: h * 512 + (ch8 + 1) * 64],
                         o=qdb[:, rb * 64:(rb + 1) * 64], m=qdc[:, h * 64:(h + 1) * 64],
                         s=s_qdv.h: v.tensor_tensor(out=o, in0=i, in1=m, op=OP.mult).then_inc(s, 1))
                    t_qd = s_qdv.inc(1)
                    ps_s = pss[b2]
                    p.wait("tensor", s_sev, j - 1)
                    p.em("tensor", lambda e, o=ps_s,
                         l=kfm[:, kvh * 2048 + gc * 64: kvh * 2048 + (gc + 1) * 64],
                         r=q_blk[:, h * 512 + ch8 * 64: h * 512 + (ch8 + 1) * 64],
                         s=s_bmm.h: e.matmul(o[:, :], l, r, start=True, stop=True).then_inc(s, 1))
                    t_smm = s_bmm.inc(1)
                    p.wait("vector", s_bmm, t_smm)
                    p.em("vector", lambda v, i=ps_s, m=msk[:, h * 64:(h + 1) * 64],
                         o=scb[(gc % 2) * 64:(gc % 2) * 64 + 64, rb * 64:(rb + 1) * 64], s=s_sev.h:
                         v.tensor_tensor(out=o, in0=i[:, :], in1=m, op=OP.mult).then_inc(s, 1))
                    t_sev = s_sev.inc(1)
                    ps_o = pso[b2]
                    p.wait("tensor", s_sev, t_sev)
                    p.wait("tensor", s_qdv, t_qd)
                    p.wait("tensor", s_kbf, t_kbf)
                    p.wait("tensor", s_oev, j - 1)
                    p.em("tensor", lambda e, o=ps_o,
                         l=vtm[(gc % 2) * 64:(gc % 2) * 64 + 64, (gc // 2) * 512 + kvh * 128: (gc // 2) * 512 + (kvh + 1) * 128],
                         r=scb[(gc % 2) * 64:(gc % 2) * 64 + 64, rb * 64:(rb + 1) * 64]:
                         e.matmul(o[:, :], l, r, start=True, stop=False))
                    p.em("tensor", lambda e, o=ps_o, l=kv_bf[:, h * 128:(h + 1) * 128],
                         r=qdb[:, rb * 64:(rb + 1) * 64], s=s_bmm.h:
                         e.matmul(o[:, :], l, r, start=False, stop=True).then_inc(s, 1))
                    t_omm = s_bmm.inc(1)
                    p.wait("scalar", s_bmm, t_omm)
                    p.em("scalar", lambda e, i=ps_o, o=o_blk[:, h * 512 + ch8 * 64: h * 512 + (ch8 + 1) * 64],
                         s=s_oev.h: e.copy(out=o, in_=i[:, :]).then_inc(s, 1))
                    s_oev.inc(1)
                    ktv = ktm[(gc % 2) * 64:(gc % 2) * 64 + 64,
                              kvh * 2048 + (gc // 2) * 128: kvh * 2048 + (gc // 2) * 128 + 128]
                    p.wait("vector", s_bmm, t_omm)
                    p.em("vector", lambda v, i=ktv, o=ktd[(gc % 2) * 64:(gc % 2) * 64 + 64, rb * 128:(rb + 1) * 128],
                         sc=kdc[(gc % 2) * 64:(gc % 2) * 64 + 64, h:h + 1], s=s_bktd.h:
                         v.tensor_scalar(out=o, in0=i, scalar1=sc, scalar2=None, op0=OP.mult).then_inc(s, 1))
                    t_ktd = s_bktd.inc(1)
                    ps_k = pskv[b2]
                    p.wait("tensor", s_bktd, t_ktd)
                    p.wait("tensor", s_bup, j - 1)
                    p.em("tensor", lambda e, o=ps_k,
                         l=ktd[(gc % 2) * 64:(gc % 2) * 64 + 64, rb * 128:(rb + 1) * 128],
                         r=vtm[(gc % 2) * 64:(gc % 2) * 64 + 64, (gc // 2) * 512 + kvh * 128: (gc // 2) * 512 + (kvh + 1) * 128],
                         s=s_bmm.h: e.matmul(o[:, :], l, r, start=True, stop=True).then_inc(s, 1))
                    t_kmm = s_bmm.inc(1)
                    p.wait("vector", s_bmm, t_kmm)
                    p.em("vector", lambda v, ps=ps_k, a=kv_sb[:, h * 128:(h + 1) * 128], bl=BLK[h], s=s_bup.h:
                         v.scalar_tensor_tensor(out=a, in0=a, scalar=bl, in1=ps[:, :], op0=OP.mult, op1=OP.add).then_inc(s, 1))
                    s_bup.inc(1)
            for ts in range(4):
                row = (tb * 4 + ts) * 128
                p.wait("sync", s_x2v, s_x2v.v)
                t_x1 = dma("sync", x1sb[:, :], x1_d[row:row + 128, :], s_x1l)
                p.wait("vector", s_x2w, s_x2w.v)
                for dd in range(4):
                    jj = (tb * 4 + ts) * 4 + dd
                    bb = jj % 2
                    ps = psq[bb]
                    if ts == 0 and dd == 0:
                        p.wait("tensor", s_wol, t_wo)
                        p.wait("tensor", s_oev, (tb + 1) * 128)
                    p.wait("tensor", s_x2v, jj - 1)
                    for ec in range(16):
                        p.em("tensor", lambda e, o=ps, l=o_blk[:, ec * 512 + ts * 128: ec * 512 + (ts + 1) * 128],
                             r=big_buf[:, ec * 2048 + dd * 512: ec * 2048 + (dd + 1) * 512],
                             st=(ec == 0), sp=(ec == 15), s=s_opm.h:
                             (lambda i: (i.then_inc(s, 1) if sp else None, i)[1])(e.matmul(o[:, :], l, r, start=st, stop=sp)))
                    t_om = s_opm.inc(1)
                    p.wait("vector", s_opm, t_om)
                    p.wait("vector", s_x1l, t_x1)
                    p.em("vector", lambda v, ps=ps, xi=x1sb[:, dd * 512:(dd + 1) * 512],
                         xo=x2sb[:, dd * 512:(dd + 1) * 512], s=s_x2v.h:
                         v.scalar_tensor_tensor(out=xo, in0=ps[:, :], scalar=0.0, in1=xi, op0=OP.bypass, op1=OP.add).then_inc(s, 1))
                    s_x2v.inc(1)
                p.wait("gpsimd", s_x2v, s_x2v.v)
                dma("gpsimd", x2_d[row:row + 128, :], x2sb[:, :], s_x2w)
        p.wait("gpsimd", s_x2w, s_x2w.v)
        p.barrier(bar)
        aps2rel()
        aps1rel()
        arel()

    if DO_LN1:
        emit_ln("ln1", x_in, mirror=False)
        if DEBUG and "y" in dbg:
            s_dby = S("s_dby")
            for dc in range(16):
                t = dma("sync", dbg["y"][dc * 128:(dc + 1) * 128, :], big_buf[:, dc * 2048:(dc + 1) * 2048], s_dby)
            p.wait("sync", s_dby, t)
    if DO_GU1:
        emit_gateup("g1")
        if DEBUG and "inter" in dbg:
            s_dbi = S("s_dbi")
            t = dma("sync", dbg["inter"], inter_d[0:2048, :], s_dbi)
            p.wait("sync", s_dbi, t)
    if DO_DN1:
        s_x1st = emit_down("d1", x_in, x1_d)
        if DEBUG:
            s_db1 = S("s_db1")
            p.wait("sync", s_x1st, s_x1st.v)
            t = dma("sync", dbg["x1"], x1_d, s_db1)
            p.wait("sync", s_db1, t)

    if STAGE >= 3:
        emit_ln("ln2", x1_d, mirror=True)
        emit_attention()
        if DEBUG:
            s_db2 = S("s_db2")
            t = dma("sync", dbg["x2"], x2_d, s_db2)
            p.wait("sync", s_db2, t)

    if STAGE >= 4:
        emit_ln("ln3", x2_d, mirror=False)
        emit_gateup("g2")
        s_ost = emit_down("d2", x2_d, out)
        p.wait("sync", s_ost, s_ost.v)

    p.barrier(bar)

    with nc.Block() as block:
        @block.sync
        def _(e):
            for f in p.q["sync"]:
                f(e)

        @block.vector
        def _(e):
            for f in p.q["vector"]:
                f(e)

        @block.scalar
        def _(e):
            for f in p.q["scalar"]:
                f(e)

        @block.tensor
        def _(e):
            for f in p.q["tensor"]:
                f(e)

        @block.gpsimd
        def _(e):
            for f in p.q["gpsimd"]:
                f(e)

    es.close()
    return nc


_lam = 2.0 ** (-8.0 * (np.arange(1, NH + 1)) / NH)
BLK = [float(np.exp(-_lam[h] * CHUNK)) for h in range(NH)]



def _host_constants(core):
    bf = ml_dtypes.bfloat16
    lam = _lam.astype(np.float64)
    pos0 = (core % 4) * T
    pos = np.arange(pos0, pos0 + T, dtype=np.float64)
    inv = 1.0 / 10000.0 ** (np.arange(0, DH, 2, dtype=np.float64) / DH)
    fr = np.concatenate([inv, inv])[:, None] * pos[None, :]
    cos_t = np.cos(fr).astype(np.float32)
    sin_t = np.sin(fr).astype(np.float32)
    r = np.arange(CHUNK, dtype=np.float64)
    diff = r[:, None] - r[None, :]
    maskT = np.zeros((CHUNK, NH * CHUNK), np.float32)
    for h in range(NH):
        m = np.where(diff >= 0, np.exp(-lam[h] * diff), 0.0)
        maskT[:, h * CHUNK:(h + 1) * CHUNK] = m.T
    qdec = np.zeros((128, NH * CHUNK), np.float32)
    for h in range(NH):
        qdec[:, h * CHUNK:(h + 1) * CHUNK] = np.exp(-lam[h] * (r + 1.0))[None, :]
    kdec = np.zeros((128, NH), np.float32)
    for h in range(NH):
        kdec[0:64, h] = np.exp(-lam[h] * (CHUNK - 1.0 - r))
        kdec[64:128, h] = kdec[0:64, h]
    blk = np.exp(-lam * CHUNK)
    cw = np.zeros((128, NCORES * NH), np.float32)
    g0 = (core // 4) * 4
    for rnk in range(g0, core):
        w = blk ** (32.0 * (core - rnk - 1))
        cw[:, rnk * NH:(rnk + 1) * NH] = w[None, :].astype(np.float32)
    return {
        "ident": np.eye(128).astype(bf),
        "cos_t": cos_t, "sin_t": sin_t,
        "maskT": maskT.astype(bf), "qdec_t": qdec.astype(bf),
        "kdec_t": kdec, "cw_t": cw,
    }


def kernel(L0_attn_x, L0_attn_wq, L0_attn_wk, L0_attn_wv, L0_attn_wo,
           L0_ffn_gate_W_packed, L0_ffn_gate_scales, L0_ffn_up_W_packed,
           L0_ffn_up_scales, L0_ffn_down_W_packed, L0_ffn_down_scales):
    x = np.ascontiguousarray(np.asarray(L0_attn_x, dtype=np.float32).reshape(2 * 8192, D))
    wqT = np.ascontiguousarray(np.asarray(L0_attn_wq).T)
    wkT = np.ascontiguousarray(np.asarray(L0_attn_wk).T)
    wvT = np.ascontiguousarray(np.asarray(L0_attn_wv).T)
    woT = np.ascontiguousarray(np.asarray(L0_attn_wo).T)
    pg = np.asarray(L0_ffn_gate_W_packed, dtype=np.int32)
    pu = np.asarray(L0_ffn_up_W_packed, dtype=np.int32)
    pd = np.asarray(L0_ffn_down_W_packed, dtype=np.int32)
    sg = np.asarray(L0_ffn_gate_scales, dtype=np.float32)
    su = np.asarray(L0_ffn_up_scales, dtype=np.float32)
    sd = np.asarray(L0_ffn_down_scales, dtype=np.float32)

    nc = build_nc()
    in_maps = []
    for c in range(NCORES):
        m = {
            "x_in": x[c * T:(c + 1) * T],
            "wqT": wqT, "wkT": wkT, "wvT": wvT, "woT": woT,
            "pk_g": np.ascontiguousarray(pg[c * T:(c + 1) * T]),
            "pk_u": np.ascontiguousarray(pu[c * T:(c + 1) * T]),
            "pk_d": np.ascontiguousarray(pd[:, c * 1024:(c + 1) * 1024]),
            "sc_g": np.ascontiguousarray(sg[c * T:(c + 1) * T]),
            "sc_u": np.ascontiguousarray(su[c * T:(c + 1) * T]),
            "sc_d": np.ascontiguousarray(sd[:, c * 64:(c + 1) * 64]),
        }
        m.update(_host_constants(c))
        in_maps.append(m)
    res = run_bass_kernel_spmd(nc, in_maps, core_ids=list(range(NCORES)))
    outs = [res.results[c]["out"] for c in range(NCORES)]
    full = np.concatenate(outs, 0).reshape(2, 8192, D).astype(np.float32)
    if DEBUG:
        kernel._dbg = res.results
    return full
